# version 03
# baseline (speedup 1.0000x reference)
import numpy as np
import concourse.bass as bass
import concourse.mybir as mybir
from concourse.bass_utils import run_bass_kernel_spmd
from contextlib import ExitStack

F32 = mybir.dt.float32
BF16 = mybir.dt.bfloat16
I32 = mybir.dt.int32
AF = mybir.ActivationFunctionType
ALU = mybir.AluOpType
AXX = mybir.AxisListType.X

D = 2048
TOK = 2048
SK = 4096
NH = 16
HID = 5632
OCH = HID // 128
DC = D // 128
CSC = 0.08838834764831845
NEG = -1.0e30

PERM = [[0, 1, 2, 3], [4, 5, 6, 7]]

_UID = [0]


def _nm(b):
    _UID[0] += 1
    return f"{b}_{_UID[0]}"



def _stripes(wt):
    Din, C = wt.shape
    return np.ascontiguousarray(
        wt.reshape(Din // 128, 128, C // 128, 128).transpose(2, 1, 0, 3)
    ).reshape(C // 128, 128, Din)


def _woflat(wt):
    O, Do = wt.shape
    return np.ascontiguousarray(
        wt.reshape(O // 128, 128, Do // 128, 128).transpose(1, 0, 2, 3)
    ).reshape(128, (O // 128) * Do)


def _cols(v):
    return np.ascontiguousarray(v.reshape(-1, 128).T).astype(np.float32)


def _masks(h):
    mcol = np.zeros((128, 64), np.float32)
    for s in range(4):
        Q = PERM[h][s]
        for g in range(16):
            ks = PERM[0][g // 2] if g < 8 else PERM[1][(g - 8) // 2]
            if ks > Q:
                mcol[:, s * 16 + g] = NEG
    pat = np.zeros((128, 2048), np.float32)
    for r in range(4):
        for p in range(128):
            pat[p, r * 512:(r + 1) * 512] = (np.arange(512) >= (128 * r + p))
    ones = np.ones((128, 2048), np.float32)
    return mcol, (pat if h == 0 else ones), (pat if h == 1 else ones)


class Sems:
    def __init__(self, nc, es, names):
        self.h = {n: es.enter_context(nc.semaphore(n)) for n in names}
        self.c = {n: 0 for n in names}

    def inc(self, ins, n, k=1):
        ins.then_inc(self.h[n], k)
        self.c[n] += k
        return self.c[n]

    def val(self, n):
        return self.c[n]

    def wait(self, e, n, v=None):
        v = self.c[n] if v is None else v
        if v > 0:
            e.wait_ge(self.h[n], v)


SUBA = [99]
DBG = [False]
QH_CELL = [None]


def build_nc(upto=99):
    nc = bass.Bass("TRN2", target_bir_lowering=False)

    xT = nc.dram_tensor("xT", [D, TOK], F32, kind="ExternalInput").ap()
    aqw = nc.dram_tensor("aqw", [16, 128, D], F32, kind="ExternalInput").ap()
    akw = nc.dram_tensor("akw", [1, 128, D], F32, kind="ExternalInput").ap()
    avw = nc.dram_tensor("avw", [1, 128, D], F32, kind="ExternalInput").ap()
    aow = nc.dram_tensor("aow", [128, 16 * D], F32, kind="ExternalInput").ap()
    bqw = nc.dram_tensor("bqw", [16, 128, D], F32, kind="ExternalInput").ap()
    bkw = nc.dram_tensor("bkw", [16, 128, D], F32, kind="ExternalInput").ap()
    bvw = nc.dram_tensor("bvw", [D, D], F32, kind="ExternalInput").ap()
    bow = nc.dram_tensor("bow", [128, 16 * D], F32, kind="ExternalInput").ap()
    gq = nc.dram_tensor("gq", [OCH, 128, D], I32, kind="ExternalInput").ap()
    uq = nc.dram_tensor("uq", [OCH, 128, D], I32, kind="ExternalInput").ap()
    dq = nc.dram_tensor("dq", [DC, 128, HID], I32, kind="ExternalInput").ap()
    sg_d = nc.dram_tensor("sg", [128, OCH], F32, kind="ExternalInput").ap()
    su_d = nc.dram_tensor("su", [128, OCH], F32, kind="ExternalInput").ap()
    sd_d = nc.dram_tensor("sd", [128, DC], F32, kind="ExternalInput").ap()
    aqb_d = nc.dram_tensor("aqb", [128, 16], F32, kind="ExternalInput").ap()
    akb_d = nc.dram_tensor("akb", [128, 1], F32, kind="ExternalInput").ap()
    avb_d = nc.dram_tensor("avb", [1, 128], F32, kind="ExternalInput").ap()
    aob_d = nc.dram_tensor("aob", [128, 16], F32, kind="ExternalInput").ap()
    bbo_d = nc.dram_tensor("bbo", [128, 16], F32, kind="ExternalInput").ap()
    mcol_d = nc.dram_tensor("mcol", [128, 64], F32, kind="ExternalInput").ap()
    ma_d = nc.dram_tensor("ma", [128, 2048], F32, kind="ExternalInput").ap()
    mb_d = nc.dram_tensor("mb", [128, 2048], F32, kind="ExternalInput").ap()

    outT = nc.dram_tensor("outT", [D, TOK], F32, kind="ExternalOutput").ap()

    ag_ak_i = nc.dram_tensor("ag_ak_i", [128, TOK], BF16).ap()
    ag_ak_o = nc.dram_tensor("ag_ak_o", [256, TOK], BF16).ap()
    ag_av_i = nc.dram_tensor("ag_av_i", [TOK, 128], BF16).ap()
    ag_av_o = nc.dram_tensor("ag_av_o", [SK, 128], BF16).ap()
    ag_bk_i = nc.dram_tensor("ag_bk_i", [D, TOK], BF16).ap()
    ag_bk_o = nc.dram_tensor("ag_bk_o", [2 * D, TOK], BF16).ap()
    ag_bv_i = nc.dram_tensor("ag_bv_i", [TOK, D], BF16).ap()
    ag_bv_o = nc.dram_tensor("ag_bv_o", [SK, D], BF16).ap()
    o_dram = nc.dram_tensor("o_dram", [D, TOK], BF16).ap()
    wg_c = nc.dram_tensor("wg_c", [OCH, 128, D], BF16).ap()
    wu_c = nc.dram_tensor("wu_c", [OCH, 128, D], BF16).ap()
    wd_c = nc.dram_tensor("wd_c", [DC, 128, HID], BF16).ap()

    RG = [[0, 1], [2, 3], [4, 5], [6, 7]]

    if DBG[0]:
        QH_CELL[0] = nc.dram_tensor("qh_scr", [128, TOK], BF16).ap()
        dbg_o = nc.dram_tensor("dbg_o", [D, TOK], F32, kind="ExternalOutput").ap()
        dbg_sc = nc.dram_tensor("dbg_sc", [1, 16], F32, kind="ExternalOutput").ap()
        dbg_st = nc.dram_tensor("dbg_st", [65, 2048], F32, kind="ExternalOutput").ap()
        dbg_kh = nc.dram_tensor("dbg_kh", [128, SK], F32, kind="ExternalOutput").ap()
        dbg_vh = nc.dram_tensor("dbg_vh", [128, SK], F32, kind="ExternalOutput").ap()
        dbg_qh = nc.dram_tensor("dbg_qh", [128, TOK], F32, kind="ExternalOutput").ap()
        dbg_bt = nc.dram_tensor("dbg_bt", [128, 64], F32, kind="ExternalOutput").ap()
        dbg_e2 = nc.dram_tensor("dbg_e2", [1, 2048], F32, kind="ExternalOutput").ap()
        dbg_var = nc.dram_tensor("dbg_var", [1, 2048], F32, kind="ExternalOutput").ap()

    es = ExitStack()
    e = es.enter_context
    with es:
        x_sb = e(nc.sbuf_tensor(_nm("x_sb"), [128, DC * TOK], BF16))
        onest = e(nc.sbuf_tensor(_nm("onest"), [65, 128], F32))
        onesb = e(nc.sbuf_tensor(_nm("onesb"), [128, 1], BF16))
        stats = e(nc.sbuf_tensor(_nm("stats"), [65, 2048], F32))
        sc_sb = e(nc.sbuf_tensor(_nm("sc_sb"), [1, 16], F32))
        rec_sb = e(nc.sbuf_tensor(_nm("rec_sb"), [1, 512], F32))
        aqb_s = e(nc.sbuf_tensor(_nm("aqb_s"), [128, 16], F32))
        akb_s = e(nc.sbuf_tensor(_nm("akb_s"), [128, 1], F32))
        avb_s = e(nc.sbuf_tensor(_nm("avb_s"), [1, 128], F32))
        aob_s = e(nc.sbuf_tensor(_nm("aob_s"), [128, 16], F32))
        bbo_s = e(nc.sbuf_tensor(_nm("bbo_s"), [128, 16], F32))
        sg_s = e(nc.sbuf_tensor(_nm("sg_s"), [128, OCH], F32))
        su_s = e(nc.sbuf_tensor(_nm("su_s"), [128, OCH], F32))
        sd_s = e(nc.sbuf_tensor(_nm("sd_s"), [128, DC], F32))
        mcol_s = e(nc.sbuf_tensor(_nm("mcol_s"), [128, 64], F32))
        ma_s = e(nc.sbuf_tensor(_nm("ma_s"), [128, 2048], BF16))
        mb_s = e(nc.sbuf_tensor(_nm("mb_s"), [128, 2048], BF16))
        vb_t = e(nc.sbuf_tensor(_nm("vb_t"), [128, 128], F32))
        ncb = e(nc.sbuf_tensor(_nm("ncb"), [128, 1], F32))
        btbl = e(nc.sbuf_tensor(_nm("btbl"), [128, 64], F32))
        recb = e(nc.sbuf_tensor(_nm("recb"), [128, 512], F32))

        M = Sems(nc, es, ["bar", "dy", "dg", "pe", "act", "dve",
                          "cak", "cav", "cbk", "cbv"])
        block = e(nc.Block())

        def T(f):
            block.tensor(f)

        def SC(f):
            block.scalar(f)

        def V(f):
            block.vector(f)

        def G(f):
            block.gpsimd(f)

        def Y(f):
            block.sync(f)

        bar_n = [0]

        def barrier():
            bar_n[0] += 1
            tgt = 5 * bar_n[0]
            for reg in (T, SC, V, G, Y):
                def f(en, tgt=tgt):
                    en.sem_inc(M.h["bar"], 1)
                    en.wait_ge(M.h["bar"], tgt)
                reg(f)
            M.c["bar"] = tgt

        def xch(d, c0, n):
            return x_sb[:, d * TOK + c0:d * TOK + c0 + n]

        def p_load(g):
            for d in range(DC):
                M.inc(g.dma_start(out=x_sb[:, d * TOK:(d + 1) * TOK],
                                  in_=xT[d * 128:(d + 1) * 128, :]), "dg", 16)
            M.inc(g.dma_start(out=ma_s[:], in_=ma_d), "dg", 16)
            M.inc(g.dma_start(out=mb_s[:], in_=mb_d), "dg", 16)
        G(p_load)
        xload_dg = M.val("dg")

        def p_load_y(y):
            for dst, src in ((aqb_s, aqb_d), (akb_s, akb_d), (avb_s, avb_d),
                             (aob_s, aob_d), (bbo_s, bbo_d), (sg_s, sg_d),
                             (su_s, su_d), (sd_s, sd_d), (mcol_s, mcol_d)):
                M.inc(y.dma_start(out=dst[:], in_=src), "dy", 16)
        Y(p_load_y)

        def p_memset(v):
            v.memset(onest[:], 1.0)
            M.inc(v.memset(onesb[:], 1.0), "dve", 1)
        V(p_memset)

        const_dy = M.val("dy")
        for reg in (T, SC, V):
            def f_fence(en, const_dy=const_dy):
                M.wait(en, "dy", const_dy)
                M.wait(en, "dg", xload_dg)
            reg(f_fence)

        with nc.psum_tensor(_nm("ps_vb"), [128, 128], F32) as ps_vb:
            def f_vb_t(t):
                M.wait(t, "dy")
                M.wait(t, "dve")
                M.inc(t.matmul(ps_vb[:], onest[0:1, :], avb_s[:],
                               start=True, stop=True), "pe", 1)
            T(f_vb_t)

            def f_vb_s(s):
                M.wait(s, "pe")
                M.inc(s.copy(vb_t[:], ps_vb[:]), "act", 1)
            SC(f_vb_s)

        with nc.sbuf_tensor(_nm("dqi"), [128, 2 * HID], I32) as dqi, \
             nc.sbuf_tensor(_nm("dqo"), [128, 2 * HID], BF16) as dqo:
            jobs = [(gq, wg_c, OCH, D), (uq, wu_c, OCH, D), (dq, wd_c, DC, HID)]
            seq = [(src, dst, i, w) for (src, dst, n, w) in jobs for i in range(n)]
            dq_sub = {}
            for k, (src, dst, i, w) in enumerate(seq):
                b = k % 2

                def fy(y, src=src, i=i, w=w, b=b, k=k):
                    if k >= 2:
                        M.wait(y, "dve", dq_sub[k - 2])
                    M.inc(y.dma_start(out=dqi[:, b * HID:b * HID + w],
                                      in_=src[i]), "dy", 16)
                Y(fy)

                def fv(v, w=w, b=b, k=k):
                    M.wait(v, "dy")
                    dq_sub[k] = M.inc(v.tensor_scalar(dqo[:, b * HID:b * HID + w],
                                                      dqi[:, b * HID:b * HID + w],
                                                      128.0, None, ALU.subtract), "dve", 1)
                V(fv)

                def fy2(y, dst=dst, i=i, w=w, b=b):
                    M.wait(y, "dve")
                    M.inc(y.dma_start(out=dst[i], in_=dqo[:, b * HID:b * HID + w]),
                          "dy", 16)
                Y(fy2)
        barrier()

        stats_dbg = [DBG[0]]

        def emit_stats():
            with nc.psum_tensor(_nm("ps_m"), [1, 2048], F32) as ps_m, \
                 nc.psum_tensor(_nm("ps_q"), [1, 2048], F32) as ps_q, \
                 nc.sbuf_tensor(_nm("sqb"), [128, 2 * TOK], BF16) as sqb:
                pe_sq = {}
                for d in range(DC):
                    b = d % 2

                    def fs(s, d=d, b=b):
                        if d == 0:
                            M.wait(s, "dg", xload_dg)
                            M.wait(s, "dve")
                        if d >= 2:
                            M.wait(s, "pe", pe_sq[d - 2])
                        M.inc(s.activation(sqb[:, b * TOK:(b + 1) * TOK],
                                           x_sb[:, d * TOK:(d + 1) * TOK],
                                           AF.Square), "act", 1)
                    SC(fs)

                    def ft(t, d=d, b=b):
                        M.wait(t, "act")
                        for tb in range(4):
                            t.matmul(ps_m[:, tb * 512:(tb + 1) * 512], onesb[:],
                                     xch(d, tb * 512, 512),
                                     start=(d == 0), stop=(d == DC - 1),
                                     skip_group_check=True)
                        for tb in range(4):
                            ins = t.matmul(ps_q[:, tb * 512:(tb + 1) * 512], onesb[:],
                                           sqb[:, b * TOK + tb * 512:b * TOK + tb * 512 + 512],
                                           start=(d == 0), stop=(d == DC - 1),
                                           skip_group_check=True)
                        pe_sq[d] = M.inc(ins, "pe", 1)
                    T(ft)

                def fs2(s):
                    M.wait(s, "pe")
                    M.inc(s.activation(stats[64:65, :], ps_m[:], AF.Copy,
                                       scale=1.0 / 2048.0), "act", 1)
                SC(fs2)

                def fv2(v):
                    M.wait(v, "act")
                    M.inc(v.tensor_scalar(stats[0:1, :], ps_q[:], 1.0 / 2048.0, 1e-5,
                                          ALU.mult, ALU.add), "dve", 1)
                V(fv2)

                def fs3(s):
                    M.wait(s, "dve")
                    M.inc(s.activation(ps_m[:], stats[64:65, :], AF.Square),
                          "act", 1)
                SC(fs3)

                if stats_dbg[0]:
                    def fde(y):
                        M.wait(y, "dve")
                        M.inc(y.dma_start(out=dbg_e2, in_=stats[0:1, :]), "dy", 16)
                    Y(fde)

                def fv3(v):
                    if stats_dbg[0]:
                        M.wait(v, "dy")
                    M.wait(v, "act")
                    M.inc(v.tensor_tensor(stats[0:1, :], stats[0:1, :], ps_m[:],
                                          ALU.subtract), "dve", 1)
                V(fv3)

                if stats_dbg[0]:
                    def fdv(y):
                        M.wait(y, "dve")
                        M.inc(y.dma_start(out=dbg_var, in_=stats[0:1, :]), "dy", 16)
                    Y(fdv)

                def fs4(s):
                    if stats_dbg[0]:
                        M.wait(s, "dy")
                    M.wait(s, "dve")
                    s.activation(stats[0:1, :], stats[0:1, :], AF.Ln)
                    M.inc(s.activation(stats[0:1, :], stats[0:1, :], AF.Exp,
                                       scale=-0.5), "act", 1)
                SC(fs4)

                def fv4(v):
                    M.wait(v, "act")
                    v.tensor_scalar(ps_m[:], stats[64:65, :], -1.0, None, ALU.mult)
                    M.inc(v.tensor_tensor(stats[32:33, :], stats[0:1, :], ps_m[:],
                                          ALU.mult), "dve", 1)
                V(fv4)
            stats_dbg[0] = False

        def emit_xn(xn_sb):
            with nc.psum_tensor(_nm("ps_b1"), [128, 512], F32) as ps_b1, \
                 nc.psum_tensor(_nm("ps_b2"), [128, 512], F32) as ps_b2:
                for tb in range(4):
                    dve_prev = M.val("dve")

                    def ft(t, tb=tb, dve_prev=dve_prev):
                        M.wait(t, "dve", dve_prev)
                        M.wait(t, "act")
                        t.matmul(ps_b1[:], onest[0:1, :],
                                 stats[0:1, tb * 512:(tb + 1) * 512],
                                 start=True, stop=True)
                        M.inc(t.matmul(ps_b2[:], onest[32:33, :],
                                       stats[32:33, tb * 512:(tb + 1) * 512],
                                       start=True, stop=True), "pe", 1)
                    T(ft)

                    def fv(v, tb=tb):
                        M.wait(v, "pe")
                        for d in range(DC):
                            v.tensor_tensor(xn_sb[:, d * TOK + tb * 512:
                                                  d * TOK + (tb + 1) * 512],
                                            xch(d, tb * 512, 512),
                                            ps_b1[:], ALU.mult)
                        ins = None
                        for d in range(DC):
                            ins = v.tensor_tensor(xn_sb[:, d * TOK + tb * 512:
                                                        d * TOK + (tb + 1) * 512],
                                                  xn_sb[:, d * TOK + tb * 512:
                                                        d * TOK + (tb + 1) * 512],
                                                  ps_b2[:], ALU.add)
                        M.inc(ins, "dve", 1)
                    V(fv)

        def emit_attn(cfg):
            sub = SUBA[0]
            emit_stats()
            if sub < 2:
                barrier()
                return
            with nc.sbuf_tensor(_nm("xn_sb"), [128, DC * TOK], BF16) as xn_sb:
                emit_xn(xn_sb)
                if sub < 3:
                    barrier()
                    return

                kv_oc = cfg["kv_oc"]
                with nc.sbuf_tensor(_nm("wstr"), [128, 2 * 2048], BF16) as wstr, \
                     nc.sbuf_tensor(_nm("kst"), [128, 4 * 512], BF16) as kst:
                    with nc.psum_tensor(_nm("ps_k"), [128, 512], F32) as ps_k, \
                         nc.psum_tensor(_nm("ps_k2"), [128, 512], F32) as ps_k2:
                        kp_pe = {}
                        kjob = 0
                        for oc in range(kv_oc):
                            b = oc % 2

                            def fg(g, oc=oc, b=b):
                                if oc >= 2:
                                    M.wait(g, "pe", kp_pe[oc - 2])
                                M.inc(g.dma_start(out=wstr[:, b * 2048:(b + 1) * 2048],
                                                  in_=cfg["wk"][oc]), "dg", 16)
                            G(fg)
                            for tb in range(4):
                                pk = ps_k if tb % 2 == 0 else ps_k2
                                sb4 = kjob % 4
                                dve_b = M.val("dve")
                                dy_b = M.val("dy")

                                def ft(t, b=b, tb=tb, pk=pk, dve_b=dve_b, oc=oc):
                                    M.wait(t, "dg")
                                    M.wait(t, "dve", dve_b)
                                    ins = None
                                    for d in range(DC):
                                        ins = t.matmul(pk[:],
                                                       wstr[:, b * 2048 + d * 128:
                                                            b * 2048 + (d + 1) * 128],
                                                       xn_sb[:, d * TOK + tb * 512:
                                                             d * TOK + (tb + 1) * 512],
                                                       start=(d == 0), stop=(d == DC - 1))
                                    kp_pe[oc] = M.inc(ins, "pe", 1)
                                T(ft)

                                def fv(v, pk=pk, sb4=sb4, dy_b=dy_b):
                                    M.wait(v, "pe")
                                    M.wait(v, "dy", dy_b)
                                    if cfg["kb"] is not None:
                                        ins = v.tensor_scalar(kst[:, sb4 * 512:(sb4 + 1) * 512],
                                                              pk[:], cfg["kb"][:, 0:1], None,
                                                              ALU.add)
                                    else:
                                        ins = v.tensor_copy(kst[:, sb4 * 512:(sb4 + 1) * 512],
                                                            pk[:])
                                    M.inc(ins, "dve", 1)
                                V(fv)

                                def fy(y, oc=oc, tb=tb, sb4=sb4):
                                    M.wait(y, "dve")
                                    M.inc(y.dma_start(
                                        out=cfg["agk_i"][oc * 128:(oc + 1) * 128,
                                                         tb * 512:(tb + 1) * 512],
                                        in_=kst[:, sb4 * 512:(sb4 + 1) * 512]), "dy", 16)
                                Y(fy)
                                kjob += 1

                    ndv = max(1, cfg["dv"] // 512)
                    dvw = min(cfg["dv"], 512)
                    with nc.sbuf_tensor(_nm("vsl"), [128, 2 * 16 * 512], BF16) as vsl, \
                         ExitStack() as es_v:
                        ps_v = [es_v.enter_context(
                            nc.psum_tensor(_nm(f"ps_v{i}"), [128, 512], F32)) for i in range(8)]
                        vp_evac = {}
                        vp_pe = {}
                        kv_evac_all = M.val("dve")
                        for dvs in range(ndv):
                            b = dvs % 2

                            def fg(g, dvs=dvs, b=b):
                                if dvs >= 2:
                                    M.wait(g, "pe", vp_pe[dvs - 2])
                                for d in range(DC):
                                    if cfg["tag"] == "a":
                                        src = cfg["wv"][0][:, d * 128:(d + 1) * 128]
                                    else:
                                        src = cfg["wv"][d * 128:(d + 1) * 128,
                                                        dvs * 512:(dvs + 1) * 512]
                                    M.inc(g.dma_start(
                                        out=vsl[:, b * 8192 + d * 512:
                                                b * 8192 + d * 512 + dvw],
                                        in_=src), "dg", 16)
                            G(fg)
                            for tth in range(2):
                                prev_ev = vp_evac.get((dvs, tth - 1),
                                                      vp_evac.get((dvs - 1, 1),
                                                                  kv_evac_all))

                                def ft(t, dvs=dvs, b=b, tth=tth, prev_ev=prev_ev):
                                    M.wait(t, "dg")
                                    if prev_ev:
                                        M.wait(t, "dve", prev_ev)
                                    ins = None
                                    for d in range(DC):
                                        for tt in range(8):
                                            ins = t.matmul(
                                                ps_v[tt][:, 0:dvw],
                                                xn_sb[:, d * TOK + (tth * 8 + tt) * 128:
                                                      d * TOK + (tth * 8 + tt) * 128 + 128],
                                                vsl[:, b * 8192 + d * 512:
                                                    b * 8192 + d * 512 + dvw],
                                                start=(d == 0), stop=(d == DC - 1),
                                                skip_group_check=True)
                                    vp_pe[dvs] = M.inc(ins, "pe", 1)
                                T(ft)
                                for tt in range(8):
                                    dy_b = M.val("dy")

                                    def fev(v, tt=tt, dy_b=dy_b):
                                        M.wait(v, "pe")
                                        M.wait(v, "dy", dy_b)
                                        if cfg["vb"]:
                                            ins = v.tensor_tensor(
                                                kst[:, (tt % 4) * 512:(tt % 4) * 512 + dvw],
                                                ps_v[tt][:, 0:dvw], vb_t[:, 0:dvw], ALU.add)
                                        else:
                                            ins = v.tensor_copy(
                                                kst[:, (tt % 4) * 512:(tt % 4) * 512 + dvw],
                                                ps_v[tt][:, 0:dvw])
                                        ev = M.inc(ins, "dve", 1)
                                        return ev
                                    V(fev)
                                    vp_evac[(dvs, tth)] = M.val("dve")

                                    def fy(y, tt=tt, dvs=dvs, tth=tth):
                                        M.wait(y, "dve")
                                        M.inc(y.dma_start(
                                            out=cfg["agv_i"][(tth * 8 + tt) * 128:
                                                             (tth * 8 + tt + 1) * 128,
                                                             dvs * 512:dvs * 512 + dvw],
                                            in_=kst[:, (tt % 4) * 512:(tt % 4) * 512 + dvw]),
                                            "dy", 16)
                                    Y(fy)

                if sub < 4:
                    barrier()
                    return

                def fcoll(g):
                    M.wait(g, "dy")
                    M.inc(g.collective_compute(
                        "AllGather", ALU.bypass, replica_groups=RG,
                        ins=[cfg["agk_i"].opt()], outs=[cfg["agk_o"].opt()]),
                        cfg["ck"], 1)
                    M.inc(g.collective_compute(
                        "AllGather", ALU.bypass, replica_groups=RG,
                        ins=[cfg["agv_i"].opt()], outs=[cfg["agv_o"].opt()]),
                        cfg["cv"], 1)
                G(fcoll)

                if sub < 5:
                    barrier()
                    return

                with nc.sbuf_tensor(_nm("kh"), [128, SK], BF16) as kh, \
                     nc.sbuf_tensor(_nm("vh"), [128, SK], BF16) as vh, \
                     nc.sbuf_tensor(_nm("qh"), [128, 2 * TOK], BF16) as qh, \
                     nc.sbuf_tensor(_nm("scr"), [128, 4096], BF16) as scr, \
                     nc.sbuf_tensor(_nm("wqs"), [128, 2048], BF16) as wqs, \
                     nc.sbuf_tensor(_nm("ost"), [128, 2 * 512], BF16) as ost, \
                     nc.psum_tensor(_nm("ps_O"), [128, 512], F32) as ps_O, \
                     nc.psum_tensor(_nm("ps_rs"), [1, 512], F32) as ps_rs, \
                     nc.psum_tensor(_nm("ps_mi"), [128, 512], F32) as ps_mi, \
                     nc.psum_tensor(_nm("ps_cl"), [128, 1], F32) as ps_cl:
                    share = cfg["share_kv"]
                    qp_pe = {}
                    head_pe = {}
                    head_act = {}
                    exp_act = {}
                    av_pe = {}
                    odma = []
                    ri = [0]
                    for h in range(NH):
                        hb = 0
                        if (not share) or h == 0:
                            pe_b = head_pe.get(h - 1, 0)
                            act_b = head_act.get(h - 1, 0)

                            def fy(y, h=h, hb=hb, pe_b=pe_b, act_b=act_b):
                                M.wait(y, cfg["ck"], M.val(cfg["ck"]))
                                M.wait(y, cfg["cv"], M.val(cfg["cv"]))
                                if pe_b:
                                    M.wait(y, "pe", pe_b)
                                if act_b:
                                    M.wait(y, "act", act_b)
                                for r in range(2):
                                    if share:
                                        ksrc = cfg["agk_o"][r * 128:(r + 1) * 128, :]
                                    else:
                                        ksrc = cfg["agk_o"][r * D + h * 128:
                                                            r * D + (h + 1) * 128, :]
                                    M.inc(y.dma_start(out=kh[:, hb * SK + r * TOK:
                                                             hb * SK + (r + 1) * TOK],
                                                      in_=ksrc), "dy", 16)
                                for r in range(2):
                                    if share:
                                        vsrc = cfg["agv_o"].rearrange(
                                            "(r t p) j -> r p t j", r=2, p=128)[r]
                                    else:
                                        vsrc = cfg["agv_o"].rearrange(
                                            "(r t p) (hh j) -> r hh p t j",
                                            r=2, p=128, j=128)[r, h]
                                    M.inc(y.dma_start(
                                        out=vh[:, hb * SK + r * TOK:hb * SK + (r + 1) * TOK]
                                        .rearrange("p (t j) -> p t j", j=128),
                                        in_=vsrc), "dy", 16)
                            Y(fy)
                        kv_dy = M.val("dy")

                        pe_b = qp_pe.get(h - 1, 0)

                        def fg(g, h=h, pe_b=pe_b):
                            if h == 0:
                                M.wait(g, "pe")
                            elif pe_b:
                                M.wait(g, "pe", pe_b)
                            M.inc(g.dma_start(out=wqs[:, 0:2048],
                                              in_=cfg["wq"][h]), "dg", 16)
                        G(fg)
                        for tb in range(4):
                            dve_b = M.val("dve")

                            def ft(t, h=h, tb=tb, dve_b=dve_b):
                                M.wait(t, "dg")
                                if tb == 0:
                                    M.wait(t, "dve", dve_b)
                                    M.wait(t, "act")
                                else:
                                    M.wait(t, "dve")
                                ins = None
                                for d in range(DC):
                                    ins = t.matmul(ps_mi[:],
                                                   wqs[:, d * 128:(d + 1) * 128],
                                                   xn_sb[:, d * TOK + tb * 512:
                                                         d * TOK + (tb + 1) * 512],
                                                   start=(d == 0), stop=(d == DC - 1))
                                qp_pe[h] = M.inc(ins, "pe", 1)
                            T(ft)

                            def fv(v, h=h, tb=tb):
                                M.wait(v, "pe")
                                if cfg["qb"] is not None:
                                    ins = v.tensor_scalar(qh[:, (h % 2) * TOK + tb * 512:
                                                             (h % 2) * TOK + (tb + 1) * 512],
                                                          ps_mi[:], cfg["qb"][:, h:h + 1],
                                                          None, ALU.add)
                                else:
                                    ins = v.tensor_copy(qh[:, (h % 2) * TOK + tb * 512:
                                                           (h % 2) * TOK + (tb + 1) * 512],
                                                        ps_mi[:])
                                M.inc(ins, "dve", 1)
                            V(fv)

                        if DBG[0] and h == 0:
                            def fdbg(y, h=h):
                                M.wait(y, "dve")
                                M.inc(y.dma_start(out=QH_CELL[0],
                                                  in_=qh[:, (h % 2) * TOK:
                                                         (h % 2 + 1) * TOK]), "dy", 16)
                            Y(fdbg)

                        with nc.psum_tensor(_nm("ps_n"), [1, 2048], F32) as ps_n:
                            def fs(s, h=h):
                                M.wait(s, "dve")
                                M.inc(s.activation(scr[:, 0:2048],
                                                   qh[:, (h % 2) * TOK:(h % 2 + 1) * TOK],
                                                   AF.Square), "act", 1)
                            SC(fs)

                            def ft(t):
                                M.wait(t, "act")
                                ins = None
                                for tb in range(4):
                                    ins = t.matmul(ps_n[:, tb * 512:(tb + 1) * 512], onesb[:],
                                                   scr[:, tb * 512:(tb + 1) * 512],
                                                   start=True, stop=True,
                                                   skip_group_check=True)
                                M.inc(ins, "pe", 1)
                            T(ft)

                            def fv(v):
                                M.wait(v, "pe")
                                M.inc(v.reduce_max(sc_sb[0:1, 0:1], ps_n[0:1, :], axis=AXX),
                                      "dve", 1)
                            V(fv)

                            if (not share) or h == 0:
                                for half in range(2):
                                    def fs2(s, hb=hb, half=half, kv_dy=kv_dy):
                                        M.wait(s, "dy", kv_dy)
                                        M.wait(s, "dve")
                                        M.inc(s.activation(scr[:, 2048:4096],
                                                           kh[:, hb * SK + half * TOK:
                                                              hb * SK + (half + 1) * TOK],
                                                           AF.Square), "act", 1)
                                    SC(fs2)

                                    def ft2(t):
                                        M.wait(t, "act")
                                        ins = None
                                        for tb in range(4):
                                            ins = t.matmul(ps_n[:, tb * 512:(tb + 1) * 512],
                                                           onesb[:],
                                                           scr[:, 2048 + tb * 512:
                                                               2048 + (tb + 1) * 512],
                                                           start=True, stop=True,
                                                           skip_group_check=True)
                                        M.inc(ins, "pe", 1)
                                    T(ft2)

                                    def fv2(v, half=half):
                                        M.wait(v, "pe")
                                        M.inc(v.reduce_max(sc_sb[0:1, 1 + half:2 + half],
                                                           ps_n[0:1, :], axis=AXX), "dve", 1)
                                    V(fv2)

                                def fv3(v):
                                    M.inc(v.tensor_tensor(sc_sb[0:1, 5:6], sc_sb[0:1, 1:2],
                                                          sc_sb[0:1, 2:3], ALU.max), "dve", 1)
                                V(fv3)

                            def fv4(v):
                                v.tensor_tensor(sc_sb[0:1, 4:5], sc_sb[0:1, 0:1],
                                                sc_sb[0:1, 5:6], ALU.add)
                                M.inc(v.tensor_scalar(sc_sb[0:1, 4:5], sc_sb[0:1, 4:5],
                                                      -0.5 * CSC, None, ALU.mult), "dve", 1)
                            V(fv4)

                            def ft3(t):
                                M.wait(t, "dve")
                                M.wait(t, "act")
                                M.inc(t.matmul(ps_cl[:], onest[0:1, :], sc_sb[0:1, 4:5],
                                               start=True, stop=True), "pe", 1)
                            T(ft3)

                            def fs3(s):
                                M.wait(s, "pe")
                                M.inc(s.copy(ncb[:], ps_cl[:]), "act", 1)
                            SC(fs3)

                            def fv5(v):
                                M.wait(v, "act")
                                ins = None
                                for s4 in range(4):
                                    ins = v.tensor_scalar(btbl[:, s4 * 16:(s4 + 1) * 16],
                                                          mcol_s[:, s4 * 16:(s4 + 1) * 16],
                                                          ncb[:, 0:1], None, ALU.add)
                                M.inc(ins, "dve", 1)
                            V(fv5)
                        btbl_dve = M.val("dve")

                        with nc.psum_tensor(_nm("ps_sA"), [128, 1024], F32) as ps_sA, \
                             nc.psum_tensor(_nm("ps_sB"), [128, 1024], F32) as ps_sB:
                            for s4 in range(4):
                                groups = list(range(16))
                                cand = {2 * s4: ("a", 0), 2 * s4 + 1: ("a", 1),
                                        8 + 2 * s4: ("b", 0), 8 + 2 * s4 + 1: ("b", 1)}
                                slot_pre_dve = M.val("dve")
                                for gi, g in enumerate(groups):
                                    r = ri[0]
                                    psS = ps_sA if r % 2 == 0 else ps_sB
                                    first = gi == 0
                                    last = gi == len(groups) - 1

                                    def ftS(t, h=h, s4=s4, g=g, gi=gi, r=r, psS=psS,
                                            kv_dy=kv_dy, hb=hb, btbl_dve=btbl_dve):
                                        if gi == 0:
                                            M.wait(t, "dy", kv_dy)
                                            M.wait(t, "dve", btbl_dve)
                                        if r - 2 in exp_act:
                                            M.wait(t, "act", exp_act[r - 2])
                                        ins = None
                                        for j in range(2):
                                            kt = 2 * g + j
                                            ins = t.matmul(psS[:, j * 512:(j + 1) * 512],
                                                           kh[:, hb * SK + kt * 128:
                                                              hb * SK + (kt + 1) * 128],
                                                           qh[:, (h % 2) * TOK + s4 * 512:
                                                              (h % 2) * TOK + (s4 + 1) * 512],
                                                           start=True, stop=True,
                                                           skip_group_check=True)
                                        M.inc(ins, "pe", 1)
                                    T(ftS)
                                    score_pe = M.val("pe")

                                    def fsE(s, g=g, s4=s4, r=r, psS=psS, score_pe=score_pe):
                                        M.wait(s, "pe", score_pe)
                                        if r - 4 in av_pe:
                                            M.wait(s, "pe", av_pe[r - 4])
                                        exp_act[r] = M.inc(
                                            s.activation(scr[:, (r % 4) * 1024:
                                                             (r % 4 + 1) * 1024],
                                                         psS[:], AF.Exp,
                                                         bias=btbl[:, s4 * 16 + g:
                                                                   s4 * 16 + g + 1],
                                                         scale=CSC), "act", 1)
                                    SC(fsE)

                                    if g in cand:
                                        which, half = cand[g]
                                        msk = ma_s if which == "a" else mb_s

                                        def fvM(v, r=r, msk=msk, half=half):
                                            M.wait(v, "act", exp_act[r])
                                            ins = None
                                            for j in range(2):
                                                ins = v.tensor_tensor(
                                                    scr[:, (r % 4) * 1024 + j * 512:
                                                        (r % 4) * 1024 + (j + 1) * 512],
                                                    scr[:, (r % 4) * 1024 + j * 512:
                                                        (r % 4) * 1024 + (j + 1) * 512],
                                                    msk[:, (half * 2 + j) * 512:
                                                        (half * 2 + j + 1) * 512],
                                                    ALU.mult)
                                            M.inc(ins, "dve", 1)
                                        V(fvM)
                                        mask_dve = M.val("dve")
                                    else:
                                        mask_dve = None

                                    def ftAV(t, g=g, r=r, first=first, last=last,
                                             hb=hb, mask_dve=mask_dve,
                                             slot_pre_dve=slot_pre_dve):
                                        if mask_dve is not None:
                                            M.wait(t, "dve", mask_dve)
                                        else:
                                            M.wait(t, "act", exp_act[r])
                                        if first:
                                            M.wait(t, "dve", slot_pre_dve)
                                        for j in range(2):
                                            kt = 2 * g + j
                                            t.matmul(ps_O[:],
                                                     vh[:, hb * SK + kt * 128:
                                                        hb * SK + (kt + 1) * 128],
                                                     scr[:, (r % 4) * 1024 + j * 512:
                                                         (r % 4) * 1024 + (j + 1) * 512],
                                                     start=(first and j == 0),
                                                     stop=(last and j == 1),
                                                     skip_group_check=True)
                                        ins = None
                                        for j in range(2):
                                            ins = t.matmul(ps_rs[:], onesb[:],
                                                           scr[:, (r % 4) * 1024 + j * 512:
                                                               (r % 4) * 1024 + (j + 1) * 512],
                                                           start=(first and j == 0),
                                                           stop=(last and j == 1),
                                                           skip_group_check=True)
                                        av_pe[r] = M.inc(ins, "pe", 1)
                                    T(ftAV)
                                    ri[0] += 1

                                def fvR(v):
                                    M.wait(v, "pe")
                                    M.inc(v.reciprocal(rec_sb[0:1, :], ps_rs[:]), "dve", 1)
                                V(fvR)

                                def ftB(t):
                                    M.wait(t, "dve")
                                    M.inc(t.matmul(ps_mi[:], onest[0:1, :], rec_sb[0:1, :],
                                                   start=True, stop=True), "pe", 1)
                                T(ftB)

                                def fsB(s):
                                    M.wait(s, "pe")
                                    M.inc(s.copy(recb[:], ps_mi[:]), "act", 1)
                                SC(fsB)

                                ost_k = len(odma)
                                dy_b = odma[ost_k - 2] if ost_k >= 2 else 0

                                def fvO(v, s4=s4, dy_b=dy_b):
                                    M.wait(v, "act")
                                    if dy_b:
                                        M.wait(v, "dy", dy_b)
                                    M.inc(v.tensor_tensor(ost[:, (s4 % 2) * 512:
                                                              (s4 % 2 + 1) * 512],
                                                          ps_O[:], recb[:], ALU.mult),
                                          "dve", 1)
                                V(fvO)

                                def fyO(y, h=h, s4=s4):
                                    M.wait(y, "dve")
                                    odma.append(M.inc(y.dma_start(
                                        out=o_dram[h * 128:(h + 1) * 128,
                                                   s4 * 512:(s4 + 1) * 512],
                                        in_=ost[:, (s4 % 2) * 512:(s4 % 2 + 1) * 512]),
                                        "dy", 16))
                                Y(fyO)
                        head_pe[h] = M.val("pe")
                        head_act[h] = M.val("act")

            if sub < 6:
                barrier()
                return

            barrier()
            with nc.sbuf_tensor(_nm("wo_sb"), [128, 16 * D], BF16) as wo_sb, \
                 nc.sbuf_tensor(_nm("osl"), [128, 16 * 512], BF16) as osl, \
                 ExitStack() as es_o:
                ps_d = [es_o.enter_context(nc.psum_tensor(_nm(f"ps_d{i}"), [128, 512], F32))
                        for i in range(4)]

                def fgo(g):
                    for q in range(4):
                        M.inc(g.dma_start(out=wo_sb[:, q * 8192:(q + 1) * 8192],
                                          in_=cfg["wo"][:, q * 8192:(q + 1) * 8192]),
                              "dg", 16)
                G(fgo)
                dve_evac = {}
                for s4 in range(4):
                    def fy(y, s4=s4):
                        M.wait(y, "dy")
                        if s4 > 0:
                            M.wait(y, "pe")
                        M.inc(y.dma_start(
                            out=osl[:].rearrange("p (hh tk) -> p hh tk", tk=512),
                            in_=o_dram.rearrange("(hh p) tk -> p hh tk", p=128)
                                [:, :, s4 * 512:(s4 + 1) * 512]), "dy", 16)
                    Y(fy)
                    for dc in range(DC):
                        pb = ps_d[dc % 4]

                        def ft(t, s4=s4, dc=dc, pb=pb):
                            M.wait(t, "dg")
                            M.wait(t, "dy")
                            k = s4 * DC + dc
                            if k >= 4:
                                M.wait(t, "dve", dve_evac[k - 4])
                            ins = None
                            for hh in range(NH):
                                ins = t.matmul(pb[:],
                                               wo_sb[:, (hh * 16 + dc) * 128:
                                                     (hh * 16 + dc + 1) * 128],
                                               osl[:, hh * 512:(hh + 1) * 512],
                                               start=(hh == 0), stop=(hh == NH - 1))
                            M.inc(ins, "pe", 1)
                        T(ft)

                        def fv(v, s4=s4, dc=dc, pb=pb):
                            M.wait(v, "pe")
                            dve_evac[s4 * DC + dc] = M.inc(v.scalar_tensor_tensor(
                                xch(dc, s4 * 512, 512), pb[:], cfg["ob"][:, dc:dc + 1],
                                xch(dc, s4 * 512, 512), ALU.add, ALU.add), "dve", 1)
                        V(fv)
            barrier()

        def emit_ffn():
            emit_stats()
            with nc.sbuf_tensor(_nm("xnf"), [128, DC * 512], BF16) as xnf, \
                 nc.sbuf_tensor(_nm("hsb"), [128, OCH * 512], BF16) as hsb, \
                 nc.sbuf_tensor(_nm("silu"), [128, 2 * 512], BF16) as silu, \
                 nc.sbuf_tensor(_nm("wgs"), [128, 4 * 2048], BF16) as wgs, \
                 nc.sbuf_tensor(_nm("wds"), [128, 2 * HID], BF16) as wds:
                for tb in range(4):
                    with nc.psum_tensor(_nm("ps_b1"), [128, 512], F32) as ps_b1, \
                         nc.psum_tensor(_nm("ps_b2"), [128, 512], F32) as ps_b2:
                        dve_prev = M.val("dve")

                        def ft(t, tb=tb, dve_prev=dve_prev):
                            M.wait(t, "dve", dve_prev)
                            M.wait(t, "act")
                            t.matmul(ps_b1[:], onest[0:1, :],
                                     stats[0:1, tb * 512:(tb + 1) * 512],
                                     start=True, stop=True)
                            M.inc(t.matmul(ps_b2[:], onest[32:33, :],
                                           stats[32:33, tb * 512:(tb + 1) * 512],
                                           start=True, stop=True), "pe", 1)
                        T(ft)

                        def fv(v, tb=tb):
                            M.wait(v, "pe")
                            for d in range(DC):
                                v.tensor_tensor(xnf[:, d * 512:(d + 1) * 512],
                                                xch(d, tb * 512, 512), ps_b1[:], ALU.mult)
                            ins = None
                            for d in range(DC):
                                ins = v.tensor_tensor(xnf[:, d * 512:(d + 1) * 512],
                                                      xnf[:, d * 512:(d + 1) * 512],
                                                      ps_b2[:], ALU.add)
                            M.inc(ins, "dve", 1)
                        V(fv)
                    xnf_dve = M.val("dve")

                    with nc.psum_tensor(_nm("ps_G"), [128, 2 * 512], F32) as ps_G, \
                         nc.psum_tensor(_nm("ps_U"), [128, 2 * 512], F32) as ps_U:
                        silu_done = {}
                        hev_done = {}
                        gu_pe = {}
                        for oc in range(OCH):
                            b = oc % 2

                            def fg(g, oc=oc, b=b):
                                if oc >= 2:
                                    M.wait(g, "pe", gu_pe[oc - 2])
                                M.inc(g.dma_start(out=wgs[:, (2 * b) * 2048:
                                                          (2 * b + 1) * 2048],
                                                  in_=wg_c[oc]), "dg", 16)
                                M.inc(g.dma_start(out=wgs[:, (2 * b + 1) * 2048:
                                                          (2 * b + 2) * 2048],
                                                  in_=wu_c[oc]), "dg", 16)
                            G(fg)

                            def ft(t, oc=oc, b=b, xnf_dve=xnf_dve):
                                M.wait(t, "dg")
                                M.wait(t, "dve", xnf_dve)
                                if oc >= 2:
                                    M.wait(t, "act", silu_done[oc - 2])
                                    M.wait(t, "dve", hev_done[oc - 2])
                                for d in range(DC):
                                    t.matmul(ps_G[:, b * 512:(b + 1) * 512],
                                             wgs[:, (2 * b) * 2048 + d * 128:
                                                 (2 * b) * 2048 + (d + 1) * 128],
                                             xnf[:, d * 512:(d + 1) * 512],
                                             start=(d == 0), stop=(d == DC - 1),
                                             skip_group_check=True)
                                ins = None
                                for d in range(DC):
                                    ins = t.matmul(ps_U[:, b * 512:(b + 1) * 512],
                                                   wgs[:, (2 * b + 1) * 2048 + d * 128:
                                                       (2 * b + 1) * 2048 + (d + 1) * 128],
                                                   xnf[:, d * 512:(d + 1) * 512],
                                                   start=(d == 0), stop=(d == DC - 1),
                                                   skip_group_check=True)
                                gu_pe[oc] = M.inc(ins, "pe", 1)
                            T(ft)

                            def fs(s, oc=oc, b=b):
                                M.wait(s, "pe")
                                silu_done[oc] = M.inc(
                                    s.activation(silu[:, b * 512:(b + 1) * 512],
                                                 ps_G[:, b * 512:(b + 1) * 512],
                                                 AF.Silu, scale=sg_s[:, oc:oc + 1]),
                                    "act", 1)
                            SC(fs)

                            def fv(v, oc=oc, b=b):
                                M.wait(v, "act")
                                hev_done[oc] = M.inc(v.scalar_tensor_tensor(
                                    hsb[:, oc * 512:(oc + 1) * 512],
                                    ps_U[:, b * 512:(b + 1) * 512],
                                    su_s[:, oc:oc + 1],
                                    silu[:, b * 512:(b + 1) * 512],
                                    ALU.mult, ALU.mult), "dve", 1)
                            V(fv)

                    with ExitStack() as es_d:
                        ps_Dn = [es_d.enter_context(
                            nc.psum_tensor(_nm(f"ps_dn{i}"), [128, 512], F32)) for i in range(4)]
                        h_dve = M.val("dve")
                        dn_evac = {}
                        dn_pe = {}
                        for dc in range(DC):
                            b = dc % 2

                            def fg(g, dc=dc, b=b):
                                if dc >= 2:
                                    M.wait(g, "pe", dn_pe[dc - 2])
                                M.inc(g.dma_start(out=wds[:, b * HID:(b + 1) * HID],
                                                  in_=wd_c[dc]), "dg", 16)
                            G(fg)

                            def ft(t, dc=dc, b=b, h_dve=h_dve):
                                M.wait(t, "dg")
                                M.wait(t, "dve", h_dve)
                                if dc >= 4:
                                    M.wait(t, "dve", dn_evac[dc - 4])
                                ins = None
                                for oc in range(OCH):
                                    ins = t.matmul(ps_Dn[dc % 4][:],
                                                   wds[:, b * HID + oc * 128:
                                                       b * HID + (oc + 1) * 128],
                                                   hsb[:, oc * 512:(oc + 1) * 512],
                                                   start=(oc == 0), stop=(oc == OCH - 1))
                                dn_pe[dc] = M.inc(ins, "pe", 1)
                            T(ft)

                            def fv(v, dc=dc, tb=tb):
                                M.wait(v, "pe")
                                dn_evac[dc] = M.inc(v.scalar_tensor_tensor(
                                    xch(dc, tb * 512, 512), ps_Dn[dc % 4][:],
                                    sd_s[:, dc:dc + 1], xch(dc, tb * 512, 512),
                                    ALU.mult, ALU.add), "dve", 1)
                            V(fv)
            barrier()

        cfg_a = dict(tag="a", kv_oc=1, dv=128, share_kv=True,
                     wq=aqw, wk=akw, wv=avw, wo=aow,
                     qb=aqb_s, kb=akb_s, vb=True, ob=aob_s,
                     agk_i=ag_ak_i, agk_o=ag_ak_o, agv_i=ag_av_i, agv_o=ag_av_o,
                     ck="cak", cv="cav")
        cfg_b = dict(tag="b", kv_oc=16, dv=2048, share_kv=False,
                     wq=bqw, wk=bkw, wv=bvw, wo=bow,
                     qb=None, kb=None, vb=False, ob=bbo_s,
                     agk_i=ag_bk_i, agk_o=ag_bk_o, agv_i=ag_bv_i, agv_o=ag_bv_o,
                     ck="cbk", cv="cbv")

        if upto >= 2:
            emit_attn(cfg_a)
        if upto >= 3:
            emit_ffn()
        if upto >= 4:
            emit_attn(cfg_b)
        if upto >= 5:
            emit_ffn()

        def p_store(g):
            for d in range(DC):
                M.inc(g.dma_start(out=outT[d * 128:(d + 1) * 128, :],
                                  in_=x_sb[:, d * TOK:(d + 1) * TOK]), "dg", 16)
            if DBG[0]:
                M.inc(g.dma_start(out=dbg_o, in_=o_dram), "dg", 16)
                M.inc(g.dma_start(out=dbg_sc, in_=sc_sb[:]), "dg", 16)
                M.inc(g.dma_start(out=dbg_st, in_=stats[:]), "dg", 16)
                M.inc(g.dma_start(out=dbg_bt, in_=btbl[:]), "dg", 16)
                M.inc(g.dma_start(out=dbg_kh[:, 0:TOK], in_=ag_ak_o[0:128, :]),
                      "dg", 16)
                M.inc(g.dma_start(out=dbg_kh[:, TOK:SK], in_=ag_ak_o[128:256, :]),
                      "dg", 16)
                M.inc(g.dma_start(out=dbg_vh.rearrange("p (t j) -> p t j", j=128),
                                  in_=ag_av_o.rearrange("(t p) j -> p t j", p=128)),
                      "dg", 16)
                M.inc(g.dma_start(out=dbg_qh, in_=QH_CELL[0]), "dg", 16)
            M.wait(g, "dg")
        G(p_store)

    return nc


def kernel(**inputs):
    f32 = np.float32
    x = np.asarray(inputs["L0_attn_a_hidden_states"], f32)
    aq = np.ascontiguousarray(np.asarray(inputs["L0_attn_a_q_w"], f32).T)
    ak = np.ascontiguousarray(np.asarray(inputs["L0_attn_a_k_w"], f32).T)
    av = np.ascontiguousarray(np.asarray(inputs["L0_attn_a_v_w"], f32).T)
    ao = np.ascontiguousarray(np.asarray(inputs["L0_attn_a_o_w"], f32).T)
    bq = np.ascontiguousarray(np.asarray(inputs["L0_attn_b_Wq"], f32).T)
    bk = np.ascontiguousarray(np.asarray(inputs["L0_attn_b_Wk"], f32).T)
    bv = np.ascontiguousarray(np.asarray(inputs["L0_attn_b_Wv"], f32).T)
    bo = np.ascontiguousarray(np.asarray(inputs["L0_attn_b_Wo"], f32).T)
    gqt = np.ascontiguousarray(np.asarray(inputs["L0_ffn_wq_gate"]).T.astype(np.int32))
    uqt = np.ascontiguousarray(np.asarray(inputs["L0_ffn_wq_up"]).T.astype(np.int32))
    dqt = np.ascontiguousarray(np.asarray(inputs["L0_ffn_wq_down"]).T.astype(np.int32))

    shared = {
        "aqw": _stripes(aq), "akw": _stripes(ak), "avw": _stripes(av),
        "aow": _woflat(ao),
        "bqw": _stripes(bq), "bkw": _stripes(bk), "bvw": bv, "bow": _woflat(bo),
        "gq": _stripes(gqt), "uq": _stripes(uqt), "dq": _stripes(dqt),
        "sg": _cols(np.asarray(inputs["L0_ffn_s_gate"], f32)),
        "su": _cols(np.asarray(inputs["L0_ffn_s_up"], f32)),
        "sd": _cols(np.asarray(inputs["L0_ffn_s_down"], f32)),
        "aqb": _cols(np.asarray(inputs["L0_attn_a_q_b"], f32)),
        "akb": np.ascontiguousarray(
            np.asarray(inputs["L0_attn_a_k_b"], f32).reshape(128, 1)),
        "avb": np.ascontiguousarray(
            np.asarray(inputs["L0_attn_a_v_b"], f32).reshape(1, 128)),
        "aob": _cols(np.asarray(inputs["L0_attn_a_o_b"], f32)),
        "bbo": _cols(np.asarray(inputs["L0_attn_b_bo"], f32)),
    }

    in_maps = []
    for c in range(8):
        b, h = c // 2, c % 2
        mcol, ma, mb = _masks(h)
        rows = np.concatenate([x[b, g * 512:(g + 1) * 512, :] for g in PERM[h]], 0)
        m = dict(shared)
        m["xT"] = np.ascontiguousarray(rows.T)
        m["mcol"] = mcol
        m["ma"] = np.ascontiguousarray(ma)
        m["mb"] = np.ascontiguousarray(mb)
        in_maps.append(m)

    nc = build_nc()
    res = run_bass_kernel_spmd(nc, in_maps, core_ids=list(range(8)))

    out = np.empty((4, 4096, 2048), np.float32)
    for c in range(8):
        b, h = c // 2, c % 2
        o = np.asarray(res.results[c]["outT"]).T
        for i, g in enumerate(PERM[h]):
            out[b, g * 512:(g + 1) * 512, :] = o[i * 512:(i + 1) * 512, :]
    return out


# version 04
# speedup vs baseline: 1.5921x; 1.5921x over previous
import numpy as np
import concourse.bass as bass
import concourse.mybir as mybir
from concourse.bass_utils import run_bass_kernel_spmd
from contextlib import ExitStack

F32 = mybir.dt.float32
BF16 = mybir.dt.bfloat16
I32 = mybir.dt.int32
AF = mybir.ActivationFunctionType
ALU = mybir.AluOpType
AXX = mybir.AxisListType.X

D = 2048
TOK = 2048
SK = 4096
NH = 16
HID = 5632
OCH = HID // 128
DC = D // 128
CSC = 0.08838834764831845
NEG = -1.0e30

PERM = [[0, 1, 2, 3], [4, 5, 6, 7]]

_UID = [0]


def _nm(b):
    _UID[0] += 1
    return f"{b}_{_UID[0]}"



def _stripes(wt):
    Din, C = wt.shape
    return np.ascontiguousarray(
        wt.reshape(Din // 128, 128, C // 128, 128).transpose(2, 1, 0, 3)
    ).reshape(C // 128, 128, Din)


def _woflat(wt):
    O, Do = wt.shape
    return np.ascontiguousarray(
        wt.reshape(O // 128, 128, Do // 128, 128).transpose(1, 0, 2, 3)
    ).reshape(128, (O // 128) * Do)


def _cols(v):
    return np.ascontiguousarray(v.reshape(-1, 128).T).astype(np.float32)


def _masks(h):
    mcol = np.zeros((128, 64), np.float32)
    for s in range(4):
        Q = PERM[h][s]
        for g in range(16):
            ks = PERM[0][g // 2] if g < 8 else PERM[1][(g - 8) // 2]
            if ks > Q:
                mcol[:, s * 16 + g] = NEG
    pat = np.zeros((128, 2048), np.float32)
    for r in range(4):
        for p in range(128):
            pat[p, r * 512:(r + 1) * 512] = (np.arange(512) >= (128 * r + p))
    ones = np.ones((128, 2048), np.float32)
    return mcol, (pat if h == 0 else ones), (pat if h == 1 else ones)


class Sems:
    def __init__(self, nc, es, names):
        self.h = {n: es.enter_context(nc.semaphore(n)) for n in names}
        self.c = {n: 0 for n in names}

    def inc(self, ins, n, k=1):
        ins.then_inc(self.h[n], k)
        self.c[n] += k
        return self.c[n]

    def val(self, n):
        return self.c[n]

    def wait(self, e, n, v=None):
        v = self.c[n] if v is None else v
        if v > 0:
            e.wait_ge(self.h[n], v)


SUBA = [99]
DBG = [False]
QH_CELL = [None]


def build_nc(upto=99):
    nc = bass.Bass("TRN2", target_bir_lowering=False)

    xT = nc.dram_tensor("xT", [D, TOK], F32, kind="ExternalInput").ap()
    aqw = nc.dram_tensor("aqw", [16, 128, D], F32, kind="ExternalInput").ap()
    akw = nc.dram_tensor("akw", [1, 128, D], F32, kind="ExternalInput").ap()
    avw = nc.dram_tensor("avw", [1, 128, D], F32, kind="ExternalInput").ap()
    aow = nc.dram_tensor("aow", [128, 16 * D], F32, kind="ExternalInput").ap()
    bqw = nc.dram_tensor("bqw", [16, 128, D], F32, kind="ExternalInput").ap()
    bkw = nc.dram_tensor("bkw", [16, 128, D], F32, kind="ExternalInput").ap()
    bvw = nc.dram_tensor("bvw", [D, D], F32, kind="ExternalInput").ap()
    bow = nc.dram_tensor("bow", [128, 16 * D], F32, kind="ExternalInput").ap()
    gq = nc.dram_tensor("gq", [OCH, 128, D], I32, kind="ExternalInput").ap()
    uq = nc.dram_tensor("uq", [OCH, 128, D], I32, kind="ExternalInput").ap()
    dq = nc.dram_tensor("dq", [DC, 128, HID], I32, kind="ExternalInput").ap()
    sg_d = nc.dram_tensor("sg", [128, OCH], F32, kind="ExternalInput").ap()
    su_d = nc.dram_tensor("su", [128, OCH], F32, kind="ExternalInput").ap()
    sd_d = nc.dram_tensor("sd", [128, DC], F32, kind="ExternalInput").ap()
    aqb_d = nc.dram_tensor("aqb", [128, 16], F32, kind="ExternalInput").ap()
    akb_d = nc.dram_tensor("akb", [128, 1], F32, kind="ExternalInput").ap()
    avb_d = nc.dram_tensor("avb", [1, 128], F32, kind="ExternalInput").ap()
    aob_d = nc.dram_tensor("aob", [128, 16], F32, kind="ExternalInput").ap()
    bbo_d = nc.dram_tensor("bbo", [128, 16], F32, kind="ExternalInput").ap()
    mcol_d = nc.dram_tensor("mcol", [128, 64], F32, kind="ExternalInput").ap()
    ma_d = nc.dram_tensor("ma", [128, 2048], F32, kind="ExternalInput").ap()
    mb_d = nc.dram_tensor("mb", [128, 2048], F32, kind="ExternalInput").ap()

    outT = nc.dram_tensor("outT", [D, TOK], F32, kind="ExternalOutput").ap()

    ag_ak_i = nc.dram_tensor("ag_ak_i", [128, TOK], BF16).ap()
    ag_ak_o = nc.dram_tensor("ag_ak_o", [256, TOK], BF16).ap()
    ag_av_i = nc.dram_tensor("ag_av_i", [TOK, 128], BF16).ap()
    ag_av_o = nc.dram_tensor("ag_av_o", [SK, 128], BF16).ap()
    ag_bk_i = nc.dram_tensor("ag_bk_i", [D, TOK], BF16).ap()
    ag_bk_o = nc.dram_tensor("ag_bk_o", [2 * D, TOK], BF16).ap()
    ag_bv_i = nc.dram_tensor("ag_bv_i", [TOK, D], BF16).ap()
    ag_bv_o = nc.dram_tensor("ag_bv_o", [SK, D], BF16).ap()
    o_dram = nc.dram_tensor("o_dram", [D, TOK], BF16).ap()
    wg_c = nc.dram_tensor("wg_c", [OCH, 128, D], BF16).ap()
    wu_c = nc.dram_tensor("wu_c", [OCH, 128, D], BF16).ap()
    wd_c = nc.dram_tensor("wd_c", [DC, 128, HID], BF16).ap()

    RG = [[0, 1], [2, 3], [4, 5], [6, 7]]

    if DBG[0]:
        QH_CELL[0] = nc.dram_tensor("qh_scr", [128, TOK], BF16).ap()
        dbg_o = nc.dram_tensor("dbg_o", [D, TOK], F32, kind="ExternalOutput").ap()
        dbg_sc = nc.dram_tensor("dbg_sc", [1, 16], F32, kind="ExternalOutput").ap()
        dbg_st = nc.dram_tensor("dbg_st", [65, 2048], F32, kind="ExternalOutput").ap()
        dbg_kh = nc.dram_tensor("dbg_kh", [128, SK], F32, kind="ExternalOutput").ap()
        dbg_vh = nc.dram_tensor("dbg_vh", [128, SK], F32, kind="ExternalOutput").ap()
        dbg_qh = nc.dram_tensor("dbg_qh", [128, TOK], F32, kind="ExternalOutput").ap()
        dbg_bt = nc.dram_tensor("dbg_bt", [128, 64], F32, kind="ExternalOutput").ap()
        dbg_e2 = nc.dram_tensor("dbg_e2", [1, 2048], F32, kind="ExternalOutput").ap()
        dbg_var = nc.dram_tensor("dbg_var", [1, 2048], F32, kind="ExternalOutput").ap()

    es = ExitStack()
    e = es.enter_context
    with es:
        x_sb = e(nc.sbuf_tensor(_nm("x_sb"), [128, DC * TOK], BF16))
        onest = e(nc.sbuf_tensor(_nm("onest"), [65, 128], F32))
        onesb = e(nc.sbuf_tensor(_nm("onesb"), [128, 1], BF16))
        stats = e(nc.sbuf_tensor(_nm("stats"), [65, 2048], F32))
        sc_sb = e(nc.sbuf_tensor(_nm("sc_sb"), [1, 16], F32))
        rec_sb = e(nc.sbuf_tensor(_nm("rec_sb"), [1, 512], F32))
        aqb_s = e(nc.sbuf_tensor(_nm("aqb_s"), [128, 16], F32))
        akb_s = e(nc.sbuf_tensor(_nm("akb_s"), [128, 1], F32))
        avb_s = e(nc.sbuf_tensor(_nm("avb_s"), [1, 128], F32))
        aob_s = e(nc.sbuf_tensor(_nm("aob_s"), [128, 16], F32))
        bbo_s = e(nc.sbuf_tensor(_nm("bbo_s"), [128, 16], F32))
        sg_s = e(nc.sbuf_tensor(_nm("sg_s"), [128, OCH], F32))
        su_s = e(nc.sbuf_tensor(_nm("su_s"), [128, OCH], F32))
        sd_s = e(nc.sbuf_tensor(_nm("sd_s"), [128, DC], F32))
        mcol_s = e(nc.sbuf_tensor(_nm("mcol_s"), [128, 64], F32))
        ma_s = e(nc.sbuf_tensor(_nm("ma_s"), [128, 2048], BF16))
        mb_s = e(nc.sbuf_tensor(_nm("mb_s"), [128, 2048], BF16))
        vb_t = e(nc.sbuf_tensor(_nm("vb_t"), [128, 128], F32))
        ncb = e(nc.sbuf_tensor(_nm("ncb"), [128, 1], F32))
        btbl = e(nc.sbuf_tensor(_nm("btbl"), [128, 64], F32))
        recb = e(nc.sbuf_tensor(_nm("recb"), [128, 512], F32))

        M = Sems(nc, es, ["bar", "dy", "dg", "pe", "act", "dve",
                          "cak", "cav", "cbk", "cbv"])
        block = e(nc.Block())

        def T(f):
            block.tensor(f)

        def SC(f):
            block.scalar(f)

        def V(f):
            block.vector(f)

        def G(f):
            block.gpsimd(f)

        def Y(f):
            block.sync(f)

        bar_n = [0]

        def barrier():
            bar_n[0] += 1
            tgt = 5 * bar_n[0]
            for reg in (T, SC, V, G, Y):
                def f(en, tgt=tgt):
                    en.sem_inc(M.h["bar"], 1)
                    en.wait_ge(M.h["bar"], tgt)
                reg(f)
            M.c["bar"] = tgt

        def xch(d, c0, n):
            return x_sb[:, d * TOK + c0:d * TOK + c0 + n]

        def p_load(g):
            for d in range(DC):
                M.inc(g.dma_start(out=x_sb[:, d * TOK:(d + 1) * TOK],
                                  in_=xT[d * 128:(d + 1) * 128, :]), "dg", 16)
            M.inc(g.dma_start(out=ma_s[:], in_=ma_d), "dg", 16)
            M.inc(g.dma_start(out=mb_s[:], in_=mb_d), "dg", 16)
        G(p_load)
        xload_dg = M.val("dg")

        def p_load_y(y):
            for dst, src in ((aqb_s, aqb_d), (akb_s, akb_d), (avb_s, avb_d),
                             (aob_s, aob_d), (bbo_s, bbo_d), (sg_s, sg_d),
                             (su_s, su_d), (sd_s, sd_d), (mcol_s, mcol_d)):
                M.inc(y.dma_start(out=dst[:], in_=src), "dy", 16)
        Y(p_load_y)

        def p_memset(v):
            v.memset(onest[:], 1.0)
            M.inc(v.memset(onesb[:], 1.0), "dve", 1)
        V(p_memset)

        const_dy = M.val("dy")
        for reg in (T, SC, V):
            def f_fence(en, const_dy=const_dy):
                M.wait(en, "dy", const_dy)
                M.wait(en, "dg", xload_dg)
            reg(f_fence)

        with nc.psum_tensor(_nm("ps_vb"), [128, 128], F32) as ps_vb:
            def f_vb_t(t):
                M.wait(t, "dy")
                M.wait(t, "dve")
                M.inc(t.matmul(ps_vb[:], onest[0:1, :], avb_s[:],
                               start=True, stop=True), "pe", 1)
            T(f_vb_t)

            def f_vb_s(s):
                M.wait(s, "pe")
                M.inc(s.copy(vb_t[:], ps_vb[:]), "act", 1)
            SC(f_vb_s)

        with nc.sbuf_tensor(_nm("dqi"), [128, 2 * HID], I32) as dqi, \
             nc.sbuf_tensor(_nm("dqo"), [128, 2 * HID], BF16) as dqo:
            jobs = [(gq, wg_c, OCH, D), (uq, wu_c, OCH, D), (dq, wd_c, DC, HID)]
            seq = [(src, dst, i, w) for (src, dst, n, w) in jobs for i in range(n)]
            dq_sub = {}
            for k, (src, dst, i, w) in enumerate(seq):
                b = k % 2

                def fy(y, src=src, i=i, w=w, b=b, k=k):
                    if k >= 2:
                        M.wait(y, "dve", dq_sub[k - 2])
                    M.inc(y.dma_start(out=dqi[:, b * HID:b * HID + w],
                                      in_=src[i]), "dy", 16)
                Y(fy)

                def fv(v, w=w, b=b, k=k):
                    M.wait(v, "dy")
                    dq_sub[k] = M.inc(v.tensor_scalar(dqo[:, b * HID:b * HID + w],
                                                      dqi[:, b * HID:b * HID + w],
                                                      128.0, None, ALU.subtract), "dve", 1)
                V(fv)

                def fy2(y, dst=dst, i=i, w=w, b=b):
                    M.wait(y, "dve")
                    M.inc(y.dma_start(out=dst[i], in_=dqo[:, b * HID:b * HID + w]),
                          "dy", 16)
                Y(fy2)
        barrier()

        stats_dbg = [DBG[0]]

        def emit_stats():
            with nc.psum_tensor(_nm("ps_m"), [1, 2048], F32) as ps_m, \
                 nc.psum_tensor(_nm("ps_q"), [1, 2048], F32) as ps_q, \
                 nc.sbuf_tensor(_nm("sqb"), [128, 2 * TOK], BF16) as sqb:
                act_b0 = M.val("act")
                pe_b0 = M.val("pe")
                sq_act = {d: act_b0 + d + 1 for d in range(DC)}
                pe_sq = {d: pe_b0 + d + 1 for d in range(DC)}

                def fs_all(s):
                    for d in range(DC):
                        if d == 0:
                            M.wait(s, "dg", xload_dg)
                            M.wait(s, "dve")
                        if d >= 2:
                            M.wait(s, "pe", pe_sq[d - 2])
                        got = M.inc(s.activation(sqb[:, (d % 2) * TOK:(d % 2 + 1) * TOK],
                                                 x_sb[:, d * TOK:(d + 1) * TOK],
                                                 AF.Square), "act", 1)
                        assert got == sq_act[d]
                SC(fs_all)

                def ft_all(t):
                    for d in range(DC):
                        M.wait(t, "act", sq_act[d])
                        for tb in range(4):
                            t.matmul(ps_m[:, tb * 512:(tb + 1) * 512], onesb[:],
                                     xch(d, tb * 512, 512),
                                     start=(d == 0), stop=(d == DC - 1),
                                     skip_group_check=True)
                        for tb in range(4):
                            ins = t.matmul(ps_q[:, tb * 512:(tb + 1) * 512], onesb[:],
                                           sqb[:, (d % 2) * TOK + tb * 512:
                                               (d % 2) * TOK + tb * 512 + 512],
                                           start=(d == 0), stop=(d == DC - 1),
                                           skip_group_check=True)
                        got = M.inc(ins, "pe", 1)
                        assert got == pe_sq[d]
                T(ft_all)

                def fs2(s):
                    M.wait(s, "pe")
                    M.inc(s.activation(stats[64:65, :], ps_m[:], AF.Copy,
                                       scale=1.0 / 2048.0), "act", 1)
                SC(fs2)

                def fv2(v):
                    M.wait(v, "act")
                    M.inc(v.tensor_scalar(stats[0:1, :], ps_q[:], 1.0 / 2048.0, 1e-5,
                                          ALU.mult, ALU.add), "dve", 1)
                V(fv2)

                def fs3(s):
                    M.wait(s, "dve")
                    M.inc(s.activation(ps_m[:], stats[64:65, :], AF.Square),
                          "act", 1)
                SC(fs3)

                if stats_dbg[0]:
                    def fde(y):
                        M.wait(y, "dve")
                        M.inc(y.dma_start(out=dbg_e2, in_=stats[0:1, :]), "dy", 16)
                    Y(fde)

                def fv3(v):
                    if stats_dbg[0]:
                        M.wait(v, "dy")
                    M.wait(v, "act")
                    M.inc(v.tensor_tensor(stats[0:1, :], stats[0:1, :], ps_m[:],
                                          ALU.subtract), "dve", 1)
                V(fv3)

                if stats_dbg[0]:
                    def fdv(y):
                        M.wait(y, "dve")
                        M.inc(y.dma_start(out=dbg_var, in_=stats[0:1, :]), "dy", 16)
                    Y(fdv)

                def fs4(s):
                    if stats_dbg[0]:
                        M.wait(s, "dy")
                    M.wait(s, "dve")
                    s.activation(stats[0:1, :], stats[0:1, :], AF.Ln)
                    M.inc(s.activation(stats[0:1, :], stats[0:1, :], AF.Exp,
                                       scale=-0.5), "act", 1)
                SC(fs4)

                def fv4(v):
                    M.wait(v, "act")
                    v.tensor_scalar(ps_m[:], stats[64:65, :], -1.0, None, ALU.mult)
                    M.inc(v.tensor_tensor(stats[32:33, :], stats[0:1, :], ps_m[:],
                                          ALU.mult), "dve", 1)
                V(fv4)
            stats_dbg[0] = False

        def emit_xn(xn_sb):
            with nc.psum_tensor(_nm("ps_b1"), [128, 512], F32) as ps_b1, \
                 nc.psum_tensor(_nm("ps_b2"), [128, 512], F32) as ps_b2:
                for tb in range(4):
                    dve_prev = M.val("dve")

                    def ft(t, tb=tb, dve_prev=dve_prev):
                        M.wait(t, "dve", dve_prev)
                        M.wait(t, "act")
                        t.matmul(ps_b1[:], onest[0:1, :],
                                 stats[0:1, tb * 512:(tb + 1) * 512],
                                 start=True, stop=True)
                        M.inc(t.matmul(ps_b2[:], onest[32:33, :],
                                       stats[32:33, tb * 512:(tb + 1) * 512],
                                       start=True, stop=True), "pe", 1)
                    T(ft)

                    def fv(v, tb=tb):
                        M.wait(v, "pe")
                        for d in range(DC):
                            v.tensor_tensor(xn_sb[:, d * TOK + tb * 512:
                                                  d * TOK + (tb + 1) * 512],
                                            xch(d, tb * 512, 512),
                                            ps_b1[:], ALU.mult)
                        ins = None
                        for d in range(DC):
                            ins = v.tensor_tensor(xn_sb[:, d * TOK + tb * 512:
                                                        d * TOK + (tb + 1) * 512],
                                                  xn_sb[:, d * TOK + tb * 512:
                                                        d * TOK + (tb + 1) * 512],
                                                  ps_b2[:], ALU.add)
                        M.inc(ins, "dve", 1)
                    V(fv)

        def emit_attn(cfg):
            sub = SUBA[0]
            emit_stats()
            if sub < 2:
                barrier()
                return
            with nc.sbuf_tensor(_nm("xn_sb"), [128, DC * TOK], BF16) as xn_sb:
                emit_xn(xn_sb)
                if sub < 3:
                    barrier()
                    return

                kv_oc = cfg["kv_oc"]
                with nc.sbuf_tensor(_nm("wstr"), [128, 2 * 2048], BF16) as wstr, \
                     nc.sbuf_tensor(_nm("kst"), [128, 4 * 512], BF16) as kst:
                    with nc.psum_tensor(_nm("ps_k"), [128, 512], F32) as ps_k, \
                         nc.psum_tensor(_nm("ps_k2"), [128, 512], F32) as ps_k2:
                        kp_pe = {}
                        kjob = 0
                        for oc in range(kv_oc):
                            b = oc % 2

                            def fg(g, oc=oc, b=b):
                                if oc >= 2:
                                    M.wait(g, "pe", kp_pe[oc - 2])
                                M.inc(g.dma_start(out=wstr[:, b * 2048:(b + 1) * 2048],
                                                  in_=cfg["wk"][oc]), "dg", 16)
                            G(fg)
                            for tb in range(4):
                                pk = ps_k if tb % 2 == 0 else ps_k2
                                sb4 = kjob % 4
                                dve_b = M.val("dve")
                                dy_b = M.val("dy")

                                def ft(t, b=b, tb=tb, pk=pk, dve_b=dve_b, oc=oc):
                                    M.wait(t, "dg")
                                    M.wait(t, "dve", dve_b)
                                    ins = None
                                    for d in range(DC):
                                        ins = t.matmul(pk[:],
                                                       wstr[:, b * 2048 + d * 128:
                                                            b * 2048 + (d + 1) * 128],
                                                       xn_sb[:, d * TOK + tb * 512:
                                                             d * TOK + (tb + 1) * 512],
                                                       start=(d == 0), stop=(d == DC - 1))
                                    kp_pe[oc] = M.inc(ins, "pe", 1)
                                T(ft)

                                def fv(v, pk=pk, sb4=sb4, dy_b=dy_b):
                                    M.wait(v, "pe")
                                    M.wait(v, "dy", dy_b)
                                    if cfg["kb"] is not None:
                                        ins = v.tensor_scalar(kst[:, sb4 * 512:(sb4 + 1) * 512],
                                                              pk[:], cfg["kb"][:, 0:1], None,
                                                              ALU.add)
                                    else:
                                        ins = v.tensor_copy(kst[:, sb4 * 512:(sb4 + 1) * 512],
                                                            pk[:])
                                    M.inc(ins, "dve", 1)
                                V(fv)

                                def fy(y, oc=oc, tb=tb, sb4=sb4):
                                    M.wait(y, "dve")
                                    M.inc(y.dma_start(
                                        out=cfg["agk_i"][oc * 128:(oc + 1) * 128,
                                                         tb * 512:(tb + 1) * 512],
                                        in_=kst[:, sb4 * 512:(sb4 + 1) * 512]), "dy", 16)
                                Y(fy)
                                kjob += 1

                    ndv = max(1, cfg["dv"] // 512)
                    dvw = min(cfg["dv"], 512)
                    with nc.sbuf_tensor(_nm("vsl"), [128, 2 * 16 * 512], BF16) as vsl, \
                         ExitStack() as es_v:
                        ps_v = [es_v.enter_context(
                            nc.psum_tensor(_nm(f"ps_v{i}"), [128, 512], F32)) for i in range(8)]
                        vp_evac = {}
                        vp_pe = {}
                        kv_evac_all = M.val("dve")
                        for dvs in range(ndv):
                            b = dvs % 2

                            def fg(g, dvs=dvs, b=b):
                                if dvs >= 2:
                                    M.wait(g, "pe", vp_pe[dvs - 2])
                                for d in range(DC):
                                    if cfg["tag"] == "a":
                                        src = cfg["wv"][0][:, d * 128:(d + 1) * 128]
                                    else:
                                        src = cfg["wv"][d * 128:(d + 1) * 128,
                                                        dvs * 512:(dvs + 1) * 512]
                                    M.inc(g.dma_start(
                                        out=vsl[:, b * 8192 + d * 512:
                                                b * 8192 + d * 512 + dvw],
                                        in_=src), "dg", 16)
                            G(fg)
                            for tth in range(2):
                                prev_ev = vp_evac.get((dvs, tth - 1),
                                                      vp_evac.get((dvs - 1, 1),
                                                                  kv_evac_all))

                                def ft(t, dvs=dvs, b=b, tth=tth, prev_ev=prev_ev):
                                    M.wait(t, "dg")
                                    if prev_ev:
                                        M.wait(t, "dve", prev_ev)
                                    ins = None
                                    for d in range(DC):
                                        for tt in range(8):
                                            ins = t.matmul(
                                                ps_v[tt][:, 0:dvw],
                                                xn_sb[:, d * TOK + (tth * 8 + tt) * 128:
                                                      d * TOK + (tth * 8 + tt) * 128 + 128],
                                                vsl[:, b * 8192 + d * 512:
                                                    b * 8192 + d * 512 + dvw],
                                                start=(d == 0), stop=(d == DC - 1),
                                                skip_group_check=True)
                                    vp_pe[dvs] = M.inc(ins, "pe", 1)
                                T(ft)
                                for tt in range(8):
                                    dy_b = M.val("dy")

                                    def fev(v, tt=tt, dy_b=dy_b):
                                        M.wait(v, "pe")
                                        M.wait(v, "dy", dy_b)
                                        if cfg["vb"]:
                                            ins = v.tensor_tensor(
                                                kst[:, (tt % 4) * 512:(tt % 4) * 512 + dvw],
                                                ps_v[tt][:, 0:dvw], vb_t[:, 0:dvw], ALU.add)
                                        else:
                                            ins = v.tensor_copy(
                                                kst[:, (tt % 4) * 512:(tt % 4) * 512 + dvw],
                                                ps_v[tt][:, 0:dvw])
                                        ev = M.inc(ins, "dve", 1)
                                        return ev
                                    V(fev)
                                    vp_evac[(dvs, tth)] = M.val("dve")

                                    def fy(y, tt=tt, dvs=dvs, tth=tth):
                                        M.wait(y, "dve")
                                        M.inc(y.dma_start(
                                            out=cfg["agv_i"][(tth * 8 + tt) * 128:
                                                             (tth * 8 + tt + 1) * 128,
                                                             dvs * 512:dvs * 512 + dvw],
                                            in_=kst[:, (tt % 4) * 512:(tt % 4) * 512 + dvw]),
                                            "dy", 16)
                                    Y(fy)

                if sub < 4:
                    barrier()
                    return

                def fcoll(g):
                    M.wait(g, "dy")
                    M.inc(g.collective_compute(
                        "AllGather", ALU.bypass, replica_groups=RG,
                        ins=[cfg["agk_i"].opt()], outs=[cfg["agk_o"].opt()]),
                        cfg["ck"], 1)
                    M.inc(g.collective_compute(
                        "AllGather", ALU.bypass, replica_groups=RG,
                        ins=[cfg["agv_i"].opt()], outs=[cfg["agv_o"].opt()]),
                        cfg["cv"], 1)
                G(fcoll)

                if sub < 5:
                    barrier()
                    return

                with nc.sbuf_tensor(_nm("kh"), [128, SK], BF16) as kh, \
                     nc.sbuf_tensor(_nm("vh"), [128, SK], BF16) as vh, \
                     nc.sbuf_tensor(_nm("qh"), [128, 2 * TOK], BF16) as qh, \
                     nc.sbuf_tensor(_nm("scr"), [128, 4096], BF16) as scr, \
                     nc.sbuf_tensor(_nm("wqs"), [128, 2048], BF16) as wqs, \
                     nc.sbuf_tensor(_nm("ost"), [128, 2 * 512], BF16) as ost, \
                     nc.psum_tensor(_nm("ps_O"), [128, 512], F32) as ps_O, \
                     nc.psum_tensor(_nm("ps_rs"), [1, 512], F32) as ps_rs, \
                     nc.psum_tensor(_nm("ps_mi"), [128, 512], F32) as ps_mi, \
                     nc.psum_tensor(_nm("ps_cl"), [128, 1], F32) as ps_cl:
                    share = cfg["share_kv"]
                    qp_pe = {}
                    head_pe = {}
                    head_act = {}
                    exp_act = {}
                    av_pe = {}
                    odma = []
                    ri = [0]
                    for h in range(NH):
                        hb = 0
                        if (not share) or h == 0:
                            pe_b = head_pe.get(h - 1, 0)
                            act_b = head_act.get(h - 1, 0)

                            def fy(y, h=h, hb=hb, pe_b=pe_b, act_b=act_b):
                                M.wait(y, cfg["ck"], M.val(cfg["ck"]))
                                M.wait(y, cfg["cv"], M.val(cfg["cv"]))
                                if pe_b:
                                    M.wait(y, "pe", pe_b)
                                if act_b:
                                    M.wait(y, "act", act_b)
                                for r in range(2):
                                    if share:
                                        ksrc = cfg["agk_o"][r * 128:(r + 1) * 128, :]
                                    else:
                                        ksrc = cfg["agk_o"][r * D + h * 128:
                                                            r * D + (h + 1) * 128, :]
                                    M.inc(y.dma_start(out=kh[:, hb * SK + r * TOK:
                                                             hb * SK + (r + 1) * TOK],
                                                      in_=ksrc), "dy", 16)
                                for r in range(2):
                                    if share:
                                        vsrc = cfg["agv_o"].rearrange(
                                            "(r t p) j -> r p t j", r=2, p=128)[r]
                                    else:
                                        vsrc = cfg["agv_o"].rearrange(
                                            "(r t p) (hh j) -> r hh p t j",
                                            r=2, p=128, j=128)[r, h]
                                    M.inc(y.dma_start(
                                        out=vh[:, hb * SK + r * TOK:hb * SK + (r + 1) * TOK]
                                        .rearrange("p (t j) -> p t j", j=128),
                                        in_=vsrc), "dy", 16)
                            Y(fy)
                        kv_dy = M.val("dy")

                        pe_b = qp_pe.get(h - 1, 0)

                        def fg(g, h=h, pe_b=pe_b):
                            if h == 0:
                                M.wait(g, "pe")
                            elif pe_b:
                                M.wait(g, "pe", pe_b)
                            M.inc(g.dma_start(out=wqs[:, 0:2048],
                                              in_=cfg["wq"][h]), "dg", 16)
                        G(fg)
                        for tb in range(4):
                            dve_b = M.val("dve")

                            def ft(t, h=h, tb=tb, dve_b=dve_b):
                                M.wait(t, "dg")
                                if tb == 0:
                                    M.wait(t, "dve", dve_b)
                                    M.wait(t, "act")
                                else:
                                    M.wait(t, "dve")
                                ins = None
                                for d in range(DC):
                                    ins = t.matmul(ps_mi[:],
                                                   wqs[:, d * 128:(d + 1) * 128],
                                                   xn_sb[:, d * TOK + tb * 512:
                                                         d * TOK + (tb + 1) * 512],
                                                   start=(d == 0), stop=(d == DC - 1))
                                qp_pe[h] = M.inc(ins, "pe", 1)
                            T(ft)

                            def fv(v, h=h, tb=tb):
                                M.wait(v, "pe")
                                if cfg["qb"] is not None:
                                    ins = v.tensor_scalar(qh[:, (h % 2) * TOK + tb * 512:
                                                             (h % 2) * TOK + (tb + 1) * 512],
                                                          ps_mi[:], cfg["qb"][:, h:h + 1],
                                                          None, ALU.add)
                                else:
                                    ins = v.tensor_copy(qh[:, (h % 2) * TOK + tb * 512:
                                                           (h % 2) * TOK + (tb + 1) * 512],
                                                        ps_mi[:])
                                M.inc(ins, "dve", 1)
                            V(fv)

                        if DBG[0] and h == 0:
                            def fdbg(y, h=h):
                                M.wait(y, "dve")
                                M.inc(y.dma_start(out=QH_CELL[0],
                                                  in_=qh[:, (h % 2) * TOK:
                                                         (h % 2 + 1) * TOK]), "dy", 16)
                            Y(fdbg)

                        with nc.psum_tensor(_nm("ps_n"), [1, 2048], F32) as ps_n:
                            def fs(s, h=h):
                                M.wait(s, "dve")
                                M.inc(s.activation(scr[:, 0:2048],
                                                   qh[:, (h % 2) * TOK:(h % 2 + 1) * TOK],
                                                   AF.Square), "act", 1)
                            SC(fs)

                            def ft(t):
                                M.wait(t, "act")
                                ins = None
                                for tb in range(4):
                                    ins = t.matmul(ps_n[:, tb * 512:(tb + 1) * 512], onesb[:],
                                                   scr[:, tb * 512:(tb + 1) * 512],
                                                   start=True, stop=True,
                                                   skip_group_check=True)
                                M.inc(ins, "pe", 1)
                            T(ft)

                            def fv(v):
                                M.wait(v, "pe")
                                M.inc(v.reduce_max(sc_sb[0:1, 0:1], ps_n[0:1, :], axis=AXX),
                                      "dve", 1)
                            V(fv)

                            if (not share) or h == 0:
                                for half in range(2):
                                    def fs2(s, hb=hb, half=half, kv_dy=kv_dy):
                                        M.wait(s, "dy", kv_dy)
                                        M.wait(s, "dve")
                                        M.inc(s.activation(scr[:, 2048:4096],
                                                           kh[:, hb * SK + half * TOK:
                                                              hb * SK + (half + 1) * TOK],
                                                           AF.Square), "act", 1)
                                    SC(fs2)

                                    def ft2(t):
                                        M.wait(t, "act")
                                        ins = None
                                        for tb in range(4):
                                            ins = t.matmul(ps_n[:, tb * 512:(tb + 1) * 512],
                                                           onesb[:],
                                                           scr[:, 2048 + tb * 512:
                                                               2048 + (tb + 1) * 512],
                                                           start=True, stop=True,
                                                           skip_group_check=True)
                                        M.inc(ins, "pe", 1)
                                    T(ft2)

                                    def fv2(v, half=half):
                                        M.wait(v, "pe")
                                        M.inc(v.reduce_max(sc_sb[0:1, 1 + half:2 + half],
                                                           ps_n[0:1, :], axis=AXX), "dve", 1)
                                    V(fv2)

                                def fv3(v):
                                    M.inc(v.tensor_tensor(sc_sb[0:1, 5:6], sc_sb[0:1, 1:2],
                                                          sc_sb[0:1, 2:3], ALU.max), "dve", 1)
                                V(fv3)

                            def fv4(v):
                                v.tensor_tensor(sc_sb[0:1, 4:5], sc_sb[0:1, 0:1],
                                                sc_sb[0:1, 5:6], ALU.add)
                                M.inc(v.tensor_scalar(sc_sb[0:1, 4:5], sc_sb[0:1, 4:5],
                                                      -0.5 * CSC, None, ALU.mult), "dve", 1)
                            V(fv4)

                            def ft3(t):
                                M.wait(t, "dve")
                                M.wait(t, "act")
                                M.inc(t.matmul(ps_cl[:], onest[0:1, :], sc_sb[0:1, 4:5],
                                               start=True, stop=True), "pe", 1)
                            T(ft3)

                            def fs3(s):
                                M.wait(s, "pe")
                                M.inc(s.copy(ncb[:], ps_cl[:]), "act", 1)
                            SC(fs3)

                            def fv5(v):
                                M.wait(v, "act")
                                ins = None
                                for s4 in range(4):
                                    ins = v.tensor_scalar(btbl[:, s4 * 16:(s4 + 1) * 16],
                                                          mcol_s[:, s4 * 16:(s4 + 1) * 16],
                                                          ncb[:, 0:1], None, ALU.add)
                                M.inc(ins, "dve", 1)
                            V(fv5)
                        btbl_dve = M.val("dve")

                        with nc.psum_tensor(_nm("ps_sA"), [128, 1024], F32) as ps_sA, \
                             nc.psum_tensor(_nm("ps_sB"), [128, 1024], F32) as ps_sB:
                            for s4 in range(4):
                                groups = list(range(16))
                                cand = {2 * s4: ("a", 0), 2 * s4 + 1: ("a", 1),
                                        8 + 2 * s4: ("b", 0), 8 + 2 * s4 + 1: ("b", 1)}
                                slot_pre_dve = M.val("dve")
                                r0 = ri[0]
                                ng = len(groups)
                                act_base = M.val("act")
                                dve_base = M.val("dve")
                                mask_val = {}
                                kc = 0
                                for gi, g in enumerate(groups):
                                    exp_act[r0 + gi] = act_base + gi + 1
                                    if g in cand:
                                        kc += 1
                                        mask_val[r0 + gi] = dve_base + kc
                                score_pe = {}

                                def ftall(t, h=h, s4=s4, groups=groups, r0=r0,
                                          kv_dy=kv_dy, hb=hb, btbl_dve=btbl_dve,
                                          slot_pre_dve=slot_pre_dve, cand=cand,
                                          mask_val=mask_val, ng=ng, score_pe=score_pe):
                                    def av(gi, g):
                                        r = r0 + gi
                                        if r in mask_val:
                                            M.wait(t, "dve", mask_val[r])
                                        else:
                                            M.wait(t, "act", exp_act[r])
                                        if gi == 0:
                                            M.wait(t, "dve", slot_pre_dve)
                                        first = gi == 0
                                        last = gi == ng - 1
                                        for j in range(2):
                                            kt = 2 * g + j
                                            t.matmul(ps_O[:],
                                                     vh[:, hb * SK + kt * 128:
                                                        hb * SK + (kt + 1) * 128],
                                                     scr[:, (r % 4) * 1024 + j * 512:
                                                         (r % 4) * 1024 + (j + 1) * 512],
                                                     start=(first and j == 0),
                                                     stop=(last and j == 1),
                                                     skip_group_check=True)
                                        ins = None
                                        for j in range(2):
                                            ins = t.matmul(ps_rs[:], onesb[:],
                                                           scr[:, (r % 4) * 1024 + j * 512:
                                                               (r % 4) * 1024 + (j + 1) * 512],
                                                           start=(first and j == 0),
                                                           stop=(last and j == 1),
                                                           skip_group_check=True)
                                        av_pe[r] = M.inc(ins, "pe", 1)

                                    for gi, g in enumerate(groups):
                                        r = r0 + gi
                                        if gi == 0:
                                            M.wait(t, "dy", kv_dy)
                                            M.wait(t, "dve", btbl_dve)
                                        if exp_act.get(r - 2, 0):
                                            M.wait(t, "act", exp_act[r - 2])
                                        psS = ps_sA if r % 2 == 0 else ps_sB
                                        ins = None
                                        for j in range(2):
                                            kt = 2 * g + j
                                            ins = t.matmul(psS[:, j * 512:(j + 1) * 512],
                                                           kh[:, hb * SK + kt * 128:
                                                              hb * SK + (kt + 1) * 128],
                                                           qh[:, (h % 2) * TOK + s4 * 512:
                                                              (h % 2) * TOK + (s4 + 1) * 512],
                                                           start=True, stop=True,
                                                           skip_group_check=True)
                                        score_pe[r] = M.inc(ins, "pe", 1)
                                        if gi >= 1:
                                            av(gi - 1, groups[gi - 1])
                                    av(ng - 1, groups[ng - 1])
                                T(ftall)

                                def fsall(s, s4=s4, groups=groups, r0=r0,
                                          score_pe=score_pe):
                                    for gi, g in enumerate(groups):
                                        r = r0 + gi
                                        psS = ps_sA if r % 2 == 0 else ps_sB
                                        M.wait(s, "pe", score_pe[r])
                                        if r - 4 in av_pe:
                                            M.wait(s, "pe", av_pe[r - 4])
                                        got = M.inc(
                                            s.activation(scr[:, (r % 4) * 1024:
                                                             (r % 4 + 1) * 1024],
                                                         psS[:], AF.Exp,
                                                         bias=btbl[:, s4 * 16 + g:
                                                                   s4 * 16 + g + 1],
                                                         scale=CSC), "act", 1)
                                        assert got == exp_act[r], (got, exp_act[r])
                                SC(fsall)

                                def fvall(v, groups=groups, r0=r0, cand=cand,
                                          mask_val=mask_val):
                                    for gi, g in enumerate(groups):
                                        if g not in cand:
                                            continue
                                        r = r0 + gi
                                        which, half = cand[g]
                                        msk = ma_s if which == "a" else mb_s
                                        M.wait(v, "act", exp_act[r])
                                        ins = None
                                        for j in range(2):
                                            ins = v.tensor_tensor(
                                                scr[:, (r % 4) * 1024 + j * 512:
                                                    (r % 4) * 1024 + (j + 1) * 512],
                                                scr[:, (r % 4) * 1024 + j * 512:
                                                    (r % 4) * 1024 + (j + 1) * 512],
                                                msk[:, (half * 2 + j) * 512:
                                                    (half * 2 + j + 1) * 512],
                                                ALU.mult)
                                        got = M.inc(ins, "dve", 1)
                                        assert got == mask_val[r], (got, mask_val[r])
                                V(fvall)
                                ri[0] += ng

                                def fvR(v):
                                    M.wait(v, "pe")
                                    M.inc(v.reciprocal(rec_sb[0:1, :], ps_rs[:]), "dve", 1)
                                V(fvR)

                                def ftB(t):
                                    M.wait(t, "dve")
                                    M.inc(t.matmul(ps_mi[:], onest[0:1, :], rec_sb[0:1, :],
                                                   start=True, stop=True), "pe", 1)
                                T(ftB)

                                def fsB(s):
                                    M.wait(s, "pe")
                                    M.inc(s.copy(recb[:], ps_mi[:]), "act", 1)
                                SC(fsB)

                                ost_k = len(odma)
                                dy_b = odma[ost_k - 2] if ost_k >= 2 else 0

                                def fvO(v, s4=s4, dy_b=dy_b):
                                    M.wait(v, "act")
                                    if dy_b:
                                        M.wait(v, "dy", dy_b)
                                    M.inc(v.tensor_tensor(ost[:, (s4 % 2) * 512:
                                                              (s4 % 2 + 1) * 512],
                                                          ps_O[:], recb[:], ALU.mult),
                                          "dve", 1)
                                V(fvO)

                                def fyO(y, h=h, s4=s4):
                                    M.wait(y, "dve")
                                    odma.append(M.inc(y.dma_start(
                                        out=o_dram[h * 128:(h + 1) * 128,
                                                   s4 * 512:(s4 + 1) * 512],
                                        in_=ost[:, (s4 % 2) * 512:(s4 % 2 + 1) * 512]),
                                        "dy", 16))
                                Y(fyO)
                        head_pe[h] = M.val("pe")
                        head_act[h] = M.val("act")

            if sub < 6:
                barrier()
                return

            barrier()
            with nc.sbuf_tensor(_nm("wo_sb"), [128, 16 * D], BF16) as wo_sb, \
                 nc.sbuf_tensor(_nm("osl"), [128, 16 * 512], BF16) as osl, \
                 ExitStack() as es_o:
                ps_d = [es_o.enter_context(nc.psum_tensor(_nm(f"ps_d{i}"), [128, 512], F32))
                        for i in range(4)]

                def fgo(g):
                    for q in range(4):
                        M.inc(g.dma_start(out=wo_sb[:, q * 8192:(q + 1) * 8192],
                                          in_=cfg["wo"][:, q * 8192:(q + 1) * 8192]),
                              "dg", 16)
                G(fgo)
                dve_evac = {}
                for s4 in range(4):
                    def fy(y, s4=s4):
                        M.wait(y, "dy")
                        if s4 > 0:
                            M.wait(y, "pe")
                        M.inc(y.dma_start(
                            out=osl[:].rearrange("p (hh tk) -> p hh tk", tk=512),
                            in_=o_dram.rearrange("(hh p) tk -> p hh tk", p=128)
                                [:, :, s4 * 512:(s4 + 1) * 512]), "dy", 16)
                    Y(fy)
                    for dc in range(DC):
                        pb = ps_d[dc % 4]

                        def ft(t, s4=s4, dc=dc, pb=pb):
                            M.wait(t, "dg")
                            M.wait(t, "dy")
                            k = s4 * DC + dc
                            if k >= 4:
                                M.wait(t, "dve", dve_evac[k - 4])
                            ins = None
                            for hh in range(NH):
                                ins = t.matmul(pb[:],
                                               wo_sb[:, (hh * 16 + dc) * 128:
                                                     (hh * 16 + dc + 1) * 128],
                                               osl[:, hh * 512:(hh + 1) * 512],
                                               start=(hh == 0), stop=(hh == NH - 1))
                            M.inc(ins, "pe", 1)
                        T(ft)

                        def fv(v, s4=s4, dc=dc, pb=pb):
                            M.wait(v, "pe")
                            dve_evac[s4 * DC + dc] = M.inc(v.scalar_tensor_tensor(
                                xch(dc, s4 * 512, 512), pb[:], cfg["ob"][:, dc:dc + 1],
                                xch(dc, s4 * 512, 512), ALU.add, ALU.add), "dve", 1)
                        V(fv)
            barrier()

        def emit_ffn():
            emit_stats()
            with nc.sbuf_tensor(_nm("xnf"), [128, DC * 512], BF16) as xnf, \
                 nc.sbuf_tensor(_nm("hsb"), [128, OCH * 512], BF16) as hsb, \
                 nc.sbuf_tensor(_nm("silu"), [128, 2 * 512], BF16) as silu, \
                 nc.sbuf_tensor(_nm("wgs"), [128, 4 * 2048], BF16) as wgs, \
                 nc.sbuf_tensor(_nm("wds"), [128, 2 * HID], BF16) as wds:
                for tb in range(4):
                    with nc.psum_tensor(_nm("ps_b1"), [128, 512], F32) as ps_b1, \
                         nc.psum_tensor(_nm("ps_b2"), [128, 512], F32) as ps_b2:
                        dve_prev = M.val("dve")

                        def ft(t, tb=tb, dve_prev=dve_prev):
                            M.wait(t, "dve", dve_prev)
                            M.wait(t, "act")
                            t.matmul(ps_b1[:], onest[0:1, :],
                                     stats[0:1, tb * 512:(tb + 1) * 512],
                                     start=True, stop=True)
                            M.inc(t.matmul(ps_b2[:], onest[32:33, :],
                                           stats[32:33, tb * 512:(tb + 1) * 512],
                                           start=True, stop=True), "pe", 1)
                        T(ft)

                        def fv(v, tb=tb):
                            M.wait(v, "pe")
                            for d in range(DC):
                                v.tensor_tensor(xnf[:, d * 512:(d + 1) * 512],
                                                xch(d, tb * 512, 512), ps_b1[:], ALU.mult)
                            ins = None
                            for d in range(DC):
                                ins = v.tensor_tensor(xnf[:, d * 512:(d + 1) * 512],
                                                      xnf[:, d * 512:(d + 1) * 512],
                                                      ps_b2[:], ALU.add)
                            M.inc(ins, "dve", 1)
                        V(fv)
                    xnf_dve = M.val("dve")

                    with nc.psum_tensor(_nm("ps_G"), [128, 2 * 512], F32) as ps_G, \
                         nc.psum_tensor(_nm("ps_U"), [128, 2 * 512], F32) as ps_U:
                        silu_done = {}
                        hev_done = {}
                        gu_pe = {}
                        for c0 in range(0, OCH, 8):
                            ocs = list(range(c0, min(c0 + 8, OCH)))
                            act_b = M.val("act")
                            dve_b = M.val("dve")
                            for i, oc in enumerate(ocs):
                                silu_done[oc] = act_b + i + 1
                                hev_done[oc] = dve_b + i + 1
                            for oc in ocs:
                                b = oc % 2

                                def fg(g, oc=oc, b=b):
                                    if oc >= 2:
                                        M.wait(g, "pe", gu_pe[oc - 2])
                                    M.inc(g.dma_start(out=wgs[:, (2 * b) * 2048:
                                                              (2 * b + 1) * 2048],
                                                      in_=wg_c[oc]), "dg", 16)
                                    M.inc(g.dma_start(out=wgs[:, (2 * b + 1) * 2048:
                                                              (2 * b + 2) * 2048],
                                                      in_=wu_c[oc]), "dg", 16)
                                G(fg)

                                def ft(t, oc=oc, b=b, xnf_dve=xnf_dve):
                                    M.wait(t, "dg")
                                    M.wait(t, "dve", xnf_dve)
                                    if oc >= 2:
                                        M.wait(t, "act", silu_done[oc - 2])
                                        M.wait(t, "dve", hev_done[oc - 2])
                                    for d in range(DC):
                                        t.matmul(ps_G[:, b * 512:(b + 1) * 512],
                                                 wgs[:, (2 * b) * 2048 + d * 128:
                                                     (2 * b) * 2048 + (d + 1) * 128],
                                                 xnf[:, d * 512:(d + 1) * 512],
                                                 start=(d == 0), stop=(d == DC - 1),
                                                 skip_group_check=True)
                                    ins = None
                                    for d in range(DC):
                                        ins = t.matmul(ps_U[:, b * 512:(b + 1) * 512],
                                                       wgs[:, (2 * b + 1) * 2048 + d * 128:
                                                           (2 * b + 1) * 2048 + (d + 1) * 128],
                                                       xnf[:, d * 512:(d + 1) * 512],
                                                       start=(d == 0), stop=(d == DC - 1),
                                                       skip_group_check=True)
                                    gu_pe[oc] = M.inc(ins, "pe", 1)
                                T(ft)

                            def fs_ch(s, ocs=ocs):
                                for oc in ocs:
                                    b = oc % 2
                                    M.wait(s, "pe", gu_pe[oc])
                                    got = M.inc(
                                        s.activation(silu[:, b * 512:(b + 1) * 512],
                                                     ps_G[:, b * 512:(b + 1) * 512],
                                                     AF.Silu, scale=sg_s[:, oc:oc + 1]),
                                        "act", 1)
                                    assert got == silu_done[oc]
                            SC(fs_ch)

                            def fv_ch(v, ocs=ocs):
                                for oc in ocs:
                                    b = oc % 2
                                    M.wait(v, "act", silu_done[oc])
                                    got = M.inc(v.scalar_tensor_tensor(
                                        hsb[:, oc * 512:(oc + 1) * 512],
                                        ps_U[:, b * 512:(b + 1) * 512],
                                        su_s[:, oc:oc + 1],
                                        silu[:, b * 512:(b + 1) * 512],
                                        ALU.mult, ALU.mult), "dve", 1)
                                    assert got == hev_done[oc]
                            V(fv_ch)

                    with ExitStack() as es_d:
                        ps_Dn = [es_d.enter_context(
                            nc.psum_tensor(_nm(f"ps_dn{i}"), [128, 512], F32)) for i in range(4)]
                        h_dve = M.val("dve")
                        dn_evac = {}
                        dn_pe = {}
                        for dc in range(DC):
                            b = dc % 2

                            def fg(g, dc=dc, b=b):
                                if dc >= 2:
                                    M.wait(g, "pe", dn_pe[dc - 2])
                                M.inc(g.dma_start(out=wds[:, b * HID:(b + 1) * HID],
                                                  in_=wd_c[dc]), "dg", 16)
                            G(fg)

                            def ft(t, dc=dc, b=b, h_dve=h_dve):
                                M.wait(t, "dg")
                                M.wait(t, "dve", h_dve)
                                if dc >= 4:
                                    M.wait(t, "dve", dn_evac[dc - 4])
                                ins = None
                                for oc in range(OCH):
                                    ins = t.matmul(ps_Dn[dc % 4][:],
                                                   wds[:, b * HID + oc * 128:
                                                       b * HID + (oc + 1) * 128],
                                                   hsb[:, oc * 512:(oc + 1) * 512],
                                                   start=(oc == 0), stop=(oc == OCH - 1))
                                dn_pe[dc] = M.inc(ins, "pe", 1)
                            T(ft)

                            def fv(v, dc=dc, tb=tb):
                                M.wait(v, "pe")
                                dn_evac[dc] = M.inc(v.scalar_tensor_tensor(
                                    xch(dc, tb * 512, 512), ps_Dn[dc % 4][:],
                                    sd_s[:, dc:dc + 1], xch(dc, tb * 512, 512),
                                    ALU.mult, ALU.add), "dve", 1)
                            V(fv)
            barrier()

        cfg_a = dict(tag="a", kv_oc=1, dv=128, share_kv=True,
                     wq=aqw, wk=akw, wv=avw, wo=aow,
                     qb=aqb_s, kb=akb_s, vb=True, ob=aob_s,
                     agk_i=ag_ak_i, agk_o=ag_ak_o, agv_i=ag_av_i, agv_o=ag_av_o,
                     ck="cak", cv="cav")
        cfg_b = dict(tag="b", kv_oc=16, dv=2048, share_kv=False,
                     wq=bqw, wk=bkw, wv=bvw, wo=bow,
                     qb=None, kb=None, vb=False, ob=bbo_s,
                     agk_i=ag_bk_i, agk_o=ag_bk_o, agv_i=ag_bv_i, agv_o=ag_bv_o,
                     ck="cbk", cv="cbv")

        if upto >= 2:
            emit_attn(cfg_a)
        if upto >= 3:
            emit_ffn()
        if upto >= 4:
            emit_attn(cfg_b)
        if upto >= 5:
            emit_ffn()

        def p_store(g):
            for d in range(DC):
                M.inc(g.dma_start(out=outT[d * 128:(d + 1) * 128, :],
                                  in_=x_sb[:, d * TOK:(d + 1) * TOK]), "dg", 16)
            if DBG[0]:
                M.inc(g.dma_start(out=dbg_o, in_=o_dram), "dg", 16)
                M.inc(g.dma_start(out=dbg_sc, in_=sc_sb[:]), "dg", 16)
                M.inc(g.dma_start(out=dbg_st, in_=stats[:]), "dg", 16)
                M.inc(g.dma_start(out=dbg_bt, in_=btbl[:]), "dg", 16)
                M.inc(g.dma_start(out=dbg_kh[:, 0:TOK], in_=ag_ak_o[0:128, :]),
                      "dg", 16)
                M.inc(g.dma_start(out=dbg_kh[:, TOK:SK], in_=ag_ak_o[128:256, :]),
                      "dg", 16)
                M.inc(g.dma_start(out=dbg_vh.rearrange("p (t j) -> p t j", j=128),
                                  in_=ag_av_o.rearrange("(t p) j -> p t j", p=128)),
                      "dg", 16)
                M.inc(g.dma_start(out=dbg_qh, in_=QH_CELL[0]), "dg", 16)
            M.wait(g, "dg")
        G(p_store)

    return nc


def kernel(**inputs):
    f32 = np.float32
    x = np.asarray(inputs["L0_attn_a_hidden_states"], f32)
    aq = np.ascontiguousarray(np.asarray(inputs["L0_attn_a_q_w"], f32).T)
    ak = np.ascontiguousarray(np.asarray(inputs["L0_attn_a_k_w"], f32).T)
    av = np.ascontiguousarray(np.asarray(inputs["L0_attn_a_v_w"], f32).T)
    ao = np.ascontiguousarray(np.asarray(inputs["L0_attn_a_o_w"], f32).T)
    bq = np.ascontiguousarray(np.asarray(inputs["L0_attn_b_Wq"], f32).T)
    bk = np.ascontiguousarray(np.asarray(inputs["L0_attn_b_Wk"], f32).T)
    bv = np.ascontiguousarray(np.asarray(inputs["L0_attn_b_Wv"], f32).T)
    bo = np.ascontiguousarray(np.asarray(inputs["L0_attn_b_Wo"], f32).T)
    gqt = np.ascontiguousarray(np.asarray(inputs["L0_ffn_wq_gate"]).T.astype(np.int32))
    uqt = np.ascontiguousarray(np.asarray(inputs["L0_ffn_wq_up"]).T.astype(np.int32))
    dqt = np.ascontiguousarray(np.asarray(inputs["L0_ffn_wq_down"]).T.astype(np.int32))

    shared = {
        "aqw": _stripes(aq), "akw": _stripes(ak), "avw": _stripes(av),
        "aow": _woflat(ao),
        "bqw": _stripes(bq), "bkw": _stripes(bk), "bvw": bv, "bow": _woflat(bo),
        "gq": _stripes(gqt), "uq": _stripes(uqt), "dq": _stripes(dqt),
        "sg": _cols(np.asarray(inputs["L0_ffn_s_gate"], f32)),
        "su": _cols(np.asarray(inputs["L0_ffn_s_up"], f32)),
        "sd": _cols(np.asarray(inputs["L0_ffn_s_down"], f32)),
        "aqb": _cols(np.asarray(inputs["L0_attn_a_q_b"], f32)),
        "akb": np.ascontiguousarray(
            np.asarray(inputs["L0_attn_a_k_b"], f32).reshape(128, 1)),
        "avb": np.ascontiguousarray(
            np.asarray(inputs["L0_attn_a_v_b"], f32).reshape(1, 128)),
        "aob": _cols(np.asarray(inputs["L0_attn_a_o_b"], f32)),
        "bbo": _cols(np.asarray(inputs["L0_attn_b_bo"], f32)),
    }

    in_maps = []
    for c in range(8):
        b, h = c // 2, c % 2
        mcol, ma, mb = _masks(h)
        rows = np.concatenate([x[b, g * 512:(g + 1) * 512, :] for g in PERM[h]], 0)
        m = dict(shared)
        m["xT"] = np.ascontiguousarray(rows.T)
        m["mcol"] = mcol
        m["ma"] = np.ascontiguousarray(ma)
        m["mb"] = np.ascontiguousarray(mb)
        in_maps.append(m)

    nc = build_nc()
    res = run_bass_kernel_spmd(nc, in_maps, core_ids=list(range(8)))

    out = np.empty((4, 4096, 2048), np.float32)
    for c in range(8):
        b, h = c // 2, c % 2
        o = np.asarray(res.results[c]["outT"]).T
        for i, g in enumerate(PERM[h]):
            out[b, g * 512:(g + 1) * 512, :] = o[i * 512:(i + 1) * 512, :]
    return out


# version 07
# speedup vs baseline: 1.6711x; 1.0496x over previous
import numpy as np
import concourse.bass as bass
import concourse.mybir as mybir
from concourse.bass_utils import run_bass_kernel_spmd
from contextlib import ExitStack

F32 = mybir.dt.float32
BF16 = mybir.dt.bfloat16
I32 = mybir.dt.int32
AF = mybir.ActivationFunctionType
ALU = mybir.AluOpType
AXX = mybir.AxisListType.X

D = 2048
TOK = 2048
SK = 4096
NH = 16
HID = 5632
OCH = HID // 128
DC = D // 128
CSC = 0.08838834764831845
NEG = -1.0e30

PERM = [[0, 2, 4, 6], [1, 3, 5, 7]]

_UID = [0]


def _nm(b):
    _UID[0] += 1
    return f"{b}_{_UID[0]}"



def _stripes(wt):
    Din, C = wt.shape
    return np.ascontiguousarray(
        wt.reshape(Din // 128, 128, C // 128, 128).transpose(2, 1, 0, 3)
    ).reshape(C // 128, 128, Din)


def _woflat(wt):
    O, Do = wt.shape
    return np.ascontiguousarray(
        wt.reshape(O // 128, 128, Do // 128, 128).transpose(1, 0, 2, 3)
    ).reshape(128, (O // 128) * Do)


def _cols(v):
    return np.ascontiguousarray(v.reshape(-1, 128).T).astype(np.float32)


def _masks(h):
    mcol = np.zeros((128, 64), np.float32)
    for s in range(4):
        Q = PERM[h][s]
        for g in range(16):
            ks = PERM[0][g // 2] if g < 8 else PERM[1][(g - 8) // 2]
            if ks > Q:
                mcol[:, s * 16 + g] = NEG
    pat = np.zeros((128, 2048), np.float32)
    for r in range(4):
        for p in range(128):
            pat[p, r * 512:(r + 1) * 512] = (np.arange(512) >= (128 * r + p))
    ones = np.ones((128, 2048), np.float32)
    return mcol, (pat if h == 0 else ones), (pat if h == 1 else ones)


class Sems:
    def __init__(self, nc, es, names):
        self.h = {n: es.enter_context(nc.semaphore(n)) for n in names}
        self.c = {n: 0 for n in names}

    def inc(self, ins, n, k=1):
        ins.then_inc(self.h[n], k)
        self.c[n] += k
        return self.c[n]

    def val(self, n):
        return self.c[n]

    def wait(self, e, n, v=None):
        v = self.c[n] if v is None else v
        if v > 0:
            e.wait_ge(self.h[n], v)


SUBA = [99]
DBG = [False]
QH_CELL = [None]
DBGW = [None]


def build_nc(upto=99):
    nc = bass.Bass("TRN2", target_bir_lowering=False)

    xT = nc.dram_tensor("xT", [D, TOK], F32, kind="ExternalInput").ap()
    aqw = nc.dram_tensor("aqw", [16, 128, D], F32, kind="ExternalInput").ap()
    akw = nc.dram_tensor("akw", [1, 128, D], F32, kind="ExternalInput").ap()
    avw = nc.dram_tensor("avw", [1, 128, D], F32, kind="ExternalInput").ap()
    aow = nc.dram_tensor("aow", [128, 16 * D], F32, kind="ExternalInput").ap()
    bqw = nc.dram_tensor("bqw", [16, 128, D], F32, kind="ExternalInput").ap()
    bkw = nc.dram_tensor("bkw", [16, 128, D], F32, kind="ExternalInput").ap()
    bvw = nc.dram_tensor("bvw", [D, D], F32, kind="ExternalInput").ap()
    bow = nc.dram_tensor("bow", [128, 16 * D], F32, kind="ExternalInput").ap()
    gq = nc.dram_tensor("gq", [OCH, 128, D], I32, kind="ExternalInput").ap()
    uq = nc.dram_tensor("uq", [OCH, 128, D], I32, kind="ExternalInput").ap()
    dq = nc.dram_tensor("dq", [DC, 128, HID], I32, kind="ExternalInput").ap()
    sg_d = nc.dram_tensor("sg", [128, OCH], F32, kind="ExternalInput").ap()
    su_d = nc.dram_tensor("su", [128, OCH], F32, kind="ExternalInput").ap()
    sd_d = nc.dram_tensor("sd", [128, DC], F32, kind="ExternalInput").ap()
    aqb_d = nc.dram_tensor("aqb", [128, 16], F32, kind="ExternalInput").ap()
    akb_d = nc.dram_tensor("akb", [128, 1], F32, kind="ExternalInput").ap()
    avb_d = nc.dram_tensor("avb", [1, 128], F32, kind="ExternalInput").ap()
    aob_d = nc.dram_tensor("aob", [128, 16], F32, kind="ExternalInput").ap()
    bbo_d = nc.dram_tensor("bbo", [128, 16], F32, kind="ExternalInput").ap()
    mcol_d = nc.dram_tensor("mcol", [128, 64], F32, kind="ExternalInput").ap()
    ma_d = nc.dram_tensor("ma", [128, 2048], F32, kind="ExternalInput").ap()
    mb_d = nc.dram_tensor("mb", [128, 2048], F32, kind="ExternalInput").ap()

    outT = nc.dram_tensor("outT", [D, TOK], F32, kind="ExternalOutput").ap()

    ag_ak_i = nc.dram_tensor("ag_ak_i", [128, TOK], BF16).ap()
    ag_ak_o = nc.dram_tensor("ag_ak_o", [256, TOK], BF16).ap()
    ag_av_i = nc.dram_tensor("ag_av_i", [TOK, 128], BF16).ap()
    ag_av_o = nc.dram_tensor("ag_av_o", [SK, 128], BF16).ap()
    ag_bk_i = nc.dram_tensor("ag_bk_i", [D, TOK], BF16).ap()
    ag_bk_o = nc.dram_tensor("ag_bk_o", [2 * D, TOK], BF16).ap()
    ag_bv_i = nc.dram_tensor("ag_bv_i", [TOK, D], BF16).ap()
    ag_bv_o = nc.dram_tensor("ag_bv_o", [SK, D], BF16).ap()
    o_dram = nc.dram_tensor("o_dram", [D, TOK], BF16).ap()
    x_d = nc.dram_tensor("x_d", [D, TOK], BF16).ap()
    wg_c = nc.dram_tensor("wg_c", [OCH, 128, D], BF16).ap()
    wu_c = nc.dram_tensor("wu_c", [OCH, 128, D], BF16).ap()
    wd_c = nc.dram_tensor("wd_c", [DC, 128, HID], BF16).ap()

    RG = [[0, 1], [2, 3], [4, 5], [6, 7]]

    if DBG[0]:
        QH_CELL[0] = nc.dram_tensor("qh_scr", [128, TOK], BF16).ap()
        DBGW[0] = nc.dram_tensor("dbgw_scr", [128, 2048], BF16).ap()
        dbg_o = nc.dram_tensor("dbg_o", [D, TOK], F32, kind="ExternalOutput").ap()
        dbg_sc = nc.dram_tensor("dbg_sc", [1, 16], F32, kind="ExternalOutput").ap()
        dbg_st = nc.dram_tensor("dbg_st", [65, 2048], F32, kind="ExternalOutput").ap()
        dbg_kh = nc.dram_tensor("dbg_kh", [128, SK], F32, kind="ExternalOutput").ap()
        dbg_vh = nc.dram_tensor("dbg_vh", [128, SK], F32, kind="ExternalOutput").ap()
        dbg_qh = nc.dram_tensor("dbg_qh", [128, TOK], F32, kind="ExternalOutput").ap()
        dbg_bt = nc.dram_tensor("dbg_bt", [128, 64], F32, kind="ExternalOutput").ap()
        dbg_e2 = nc.dram_tensor("dbg_e2", [1, 2048], F32, kind="ExternalOutput").ap()
        dbg_var = nc.dram_tensor("dbg_var", [1, 2048], F32, kind="ExternalOutput").ap()

    es = ExitStack()
    e = es.enter_context
    with es:
        dqi = e(nc.sbuf_tensor(_nm("dqi"), [128, 2 * 2048], I32))
        dqo = e(nc.sbuf_tensor(_nm("dqo"), [128, 2 * 2048], BF16))
        onest = e(nc.sbuf_tensor(_nm("onest"), [65, 128], F32))
        onesb = e(nc.sbuf_tensor(_nm("onesb"), [128, 1], BF16))
        stats = e(nc.sbuf_tensor(_nm("stats"), [65, 2048], F32))
        sc_sb = e(nc.sbuf_tensor(_nm("sc_sb"), [1, 16], F32))
        rec_sb = e(nc.sbuf_tensor(_nm("rec_sb"), [1, 512], F32))
        aqb_s = e(nc.sbuf_tensor(_nm("aqb_s"), [128, 16], F32))
        akb_s = e(nc.sbuf_tensor(_nm("akb_s"), [128, 1], F32))
        avb_s = e(nc.sbuf_tensor(_nm("avb_s"), [1, 128], F32))
        aob_s = e(nc.sbuf_tensor(_nm("aob_s"), [128, 16], F32))
        bbo_s = e(nc.sbuf_tensor(_nm("bbo_s"), [128, 16], F32))
        sg_s = e(nc.sbuf_tensor(_nm("sg_s"), [128, OCH], F32))
        su_s = e(nc.sbuf_tensor(_nm("su_s"), [128, OCH], F32))
        sd_s = e(nc.sbuf_tensor(_nm("sd_s"), [128, DC], F32))
        mcol_s = e(nc.sbuf_tensor(_nm("mcol_s"), [128, 64], F32))
        ma_s = e(nc.sbuf_tensor(_nm("ma_s"), [128, 2048], BF16))
        mb_s = e(nc.sbuf_tensor(_nm("mb_s"), [128, 2048], BF16))
        vb_t = e(nc.sbuf_tensor(_nm("vb_t"), [128, 128], F32))
        ncb = e(nc.sbuf_tensor(_nm("ncb"), [128, 1], F32))
        btbl = e(nc.sbuf_tensor(_nm("btbl"), [128, 64], F32))
        recb = e(nc.sbuf_tensor(_nm("recb"), [128, 512], F32))

        M = Sems(nc, es, ["bar", "dy", "dg", "pe", "act", "dve",
                          "cak", "cav", "cbk", "cbv"])
        block = e(nc.Block())

        def T(f):
            block.tensor(f)

        def SC(f):
            block.scalar(f)

        def V(f):
            block.vector(f)

        def G(f):
            block.gpsimd(f)

        def Y(f):
            block.sync(f)

        bar_n = [0]

        def barrier():
            bar_n[0] += 1
            tgt = 5 * bar_n[0]
            for reg in (T, SC, V, G, Y):
                def f(en, tgt=tgt):
                    en.sem_inc(M.h["bar"], 1)
                    en.wait_ge(M.h["bar"], tgt)
                reg(f)
            M.c["bar"] = tgt

        def p_load(g):
            M.inc(g.dma_start(out=x_d, in_=xT), "dg", 16)
            M.inc(g.dma_start(out=ma_s[:], in_=ma_d), "dg", 16)
            M.inc(g.dma_start(out=mb_s[:], in_=mb_d), "dg", 16)
        G(p_load)
        xload_dg = M.val("dg")

        def p_load_y(y):
            for dst, src in ((aqb_s, aqb_d), (akb_s, akb_d), (avb_s, avb_d),
                             (aob_s, aob_d), (bbo_s, bbo_d), (sg_s, sg_d),
                             (su_s, su_d), (sd_s, sd_d), (mcol_s, mcol_d)):
                M.inc(y.dma_start(out=dst[:], in_=src), "dy", 16)
        Y(p_load_y)

        def p_memset(v):
            v.memset(onest[:], 1.0)
            M.inc(v.memset(onesb[:], 1.0), "dve", 1)
        V(p_memset)

        const_dy = M.val("dy")
        for reg in (T, SC, V):
            def f_fence(en, const_dy=const_dy):
                M.wait(en, "dy", const_dy)
                M.wait(en, "dg", xload_dg)
            reg(f_fence)

        with nc.psum_tensor(_nm("ps_vb"), [128, 128], F32) as ps_vb:
            def f_vb_t(t):
                M.wait(t, "dy")
                M.wait(t, "dve")
                M.inc(t.matmul(ps_vb[:], onest[0:1, :], avb_s[:],
                               start=True, stop=True), "pe", 1)
            T(f_vb_t)

            def f_vb_s(s):
                M.wait(s, "pe")
                M.inc(s.copy(vb_t[:], ps_vb[:]), "act", 1)
            SC(f_vb_s)

        dq_seq = ([(gq, wg_c, i, 0, 2048) for i in range(OCH)]
                  + [(uq, wu_c, i, 0, 2048) for i in range(OCH)]
                  + [(dq, wd_c, i, off, w) for i in range(DC)
                     for off, w in ((0, 2048), (2048, 2048), (4096, 1536))])
        dq_state = {"k": 0, "sub": {}, "done_dy": 0}

        def emit_dq_jobs(n):
            k0 = dq_state["k"]
            for k in range(k0, min(k0 + n, len(dq_seq))):
                src, dst, i, off, w = dq_seq[k]
                b = k % 2

                def fy(y, src=src, i=i, off=off, w=w, b=b, k=k):
                    if k == 0:
                        M.wait(y, "dg", xload_dg)
                    if k >= 2:
                        M.wait(y, "dve", dq_state["sub"][k - 2])
                    M.inc(y.dma_start(out=dqi[:, b * 2048:b * 2048 + w],
                                      in_=src[i][:, off:off + w]), "dy", 16)
                Y(fy)

                def fv(v, w=w, b=b, k=k):
                    M.wait(v, "dy")
                    dq_state["sub"][k] = M.inc(
                        v.tensor_scalar(dqo[:, b * 2048:b * 2048 + w],
                                        dqi[:, b * 2048:b * 2048 + w],
                                        128.0, None, ALU.subtract), "dve", 1)
                V(fv)

                def fy2(y, dst=dst, i=i, off=off, w=w, b=b):
                    M.wait(y, "dve")
                    M.inc(y.dma_start(out=dst[i][:, off:off + w],
                                      in_=dqo[:, b * 2048:b * 2048 + w]), "dy", 16)
                Y(fy2)
            dq_state["k"] = min(k0 + n, len(dq_seq))
            dq_state["done_dy"] = M.val("dy")

        import os as _os
        if not _os.environ.get("NODQ"):
            emit_dq_jobs(16)

        stats_dbg = [DBG[0]]

        def emit_stats():
            with nc.psum_tensor(_nm("ps_m"), [1, 2048], F32) as ps_m, \
                 nc.psum_tensor(_nm("ps_q"), [1, 2048], F32) as ps_q, \
                 nc.sbuf_tensor(_nm("sqb"), [128, 2 * TOK], BF16) as sqb, \
                 nc.sbuf_tensor(_nm("xring"), [128, 2 * TOK], BF16) as xring:
                act_b0 = M.val("act")
                pe_b0 = M.val("pe")
                sq_act = {d: act_b0 + d + 1 for d in range(DC)}
                pe_sq = {d: pe_b0 + d + 1 for d in range(DC)}
                x_cur_dy = M.val("dy")
                dy_ld = {}

                def fy_ring(y):
                    M.wait(y, "dg", xload_dg)
                    M.wait(y, "dy", x_cur_dy)
                    for d in range(DC):
                        if d >= 2:
                            M.wait(y, "pe", pe_sq[d - 2])
                            M.wait(y, "act", sq_act[d - 2])
                        M.inc(y.dma_start(out=xring[:, (d % 2) * TOK:(d % 2 + 1) * TOK],
                                          in_=x_d[d * 128:(d + 1) * 128, :]), "dy", 16)
                        dy_ld[d] = M.val("dy")
                Y(fy_ring)

                def fs_all(s):
                    for d in range(DC):
                        if d == 0:
                            M.wait(s, "dve")
                        if d >= 2:
                            M.wait(s, "pe", pe_sq[d - 2])
                        M.wait(s, "dy", dy_ld[d])
                        got = M.inc(s.activation(sqb[:, (d % 2) * TOK:(d % 2 + 1) * TOK],
                                                 xring[:, (d % 2) * TOK:(d % 2 + 1) * TOK],
                                                 AF.Square), "act", 1)
                        assert got == sq_act[d]
                SC(fs_all)

                def ft_all(t):
                    for d in range(DC):
                        M.wait(t, "act", sq_act[d])
                        for tb in range(4):
                            t.matmul(ps_m[:, tb * 512:(tb + 1) * 512], onesb[:],
                                     xring[:, (d % 2) * TOK + tb * 512:
                                           (d % 2) * TOK + tb * 512 + 512],
                                     start=(d == 0), stop=(d == DC - 1),
                                     skip_group_check=True)
                        for tb in range(4):
                            ins = t.matmul(ps_q[:, tb * 512:(tb + 1) * 512], onesb[:],
                                           sqb[:, (d % 2) * TOK + tb * 512:
                                               (d % 2) * TOK + tb * 512 + 512],
                                           start=(d == 0), stop=(d == DC - 1),
                                           skip_group_check=True)
                        got = M.inc(ins, "pe", 1)
                        assert got == pe_sq[d]
                T(ft_all)

                def fs2(s):
                    M.wait(s, "pe")
                    M.inc(s.activation(stats[64:65, :], ps_m[:], AF.Copy,
                                       scale=1.0 / 2048.0), "act", 1)
                SC(fs2)

                def fv2(v):
                    M.wait(v, "act")
                    M.inc(v.tensor_scalar(stats[0:1, :], ps_q[:], 1.0 / 2048.0, 1e-5,
                                          ALU.mult, ALU.add), "dve", 1)
                V(fv2)

                def fs3(s):
                    M.wait(s, "dve")
                    M.inc(s.activation(ps_m[:], stats[64:65, :], AF.Square),
                          "act", 1)
                SC(fs3)

                if stats_dbg[0]:
                    def fde(y):
                        M.wait(y, "dve")
                        M.inc(y.dma_start(out=dbg_e2, in_=stats[0:1, :]), "dy", 16)
                    Y(fde)

                def fv3(v):
                    if stats_dbg[0]:
                        M.wait(v, "dy")
                    M.wait(v, "act")
                    M.inc(v.tensor_tensor(stats[0:1, :], stats[0:1, :], ps_m[:],
                                          ALU.subtract), "dve", 1)
                V(fv3)

                if stats_dbg[0]:
                    def fdv(y):
                        M.wait(y, "dve")
                        M.inc(y.dma_start(out=dbg_var, in_=stats[0:1, :]), "dy", 16)
                    Y(fdv)

                def fs4(s):
                    if stats_dbg[0]:
                        M.wait(s, "dy")
                    M.wait(s, "dve")
                    s.activation(stats[0:1, :], stats[0:1, :], AF.Ln)
                    M.inc(s.activation(stats[0:1, :], stats[0:1, :], AF.Exp,
                                       scale=-0.5), "act", 1)
                SC(fs4)

                def fv4(v):
                    M.wait(v, "act")
                    v.tensor_scalar(ps_m[:], stats[64:65, :], -1.0, None, ALU.mult)
                    M.inc(v.tensor_tensor(stats[32:33, :], stats[0:1, :], ps_m[:],
                                          ALU.mult), "dve", 1)
                V(fv4)
            stats_dbg[0] = False

        def emit_xn(xn_sb):
            with nc.psum_tensor(_nm("ps_b1"), [128, 2048], F32) as ps_b1, \
                 nc.psum_tensor(_nm("ps_b2"), [128, 2048], F32) as ps_b2, \
                 nc.sbuf_tensor(_nm("xring"), [128, 2 * TOK], BF16) as xring:
                dve_prev = M.val("dve")
                x_cur_dy = M.val("dy")
                xn_dve = {d: dve_prev + d + 1 for d in range(DC)}
                dy_ld = {}

                def ft(t):
                    M.wait(t, "dve", dve_prev)
                    M.wait(t, "act")
                    for tb in range(4):
                        t.matmul(ps_b1[:, tb * 512:(tb + 1) * 512], onest[0:1, :],
                                 stats[0:1, tb * 512:(tb + 1) * 512],
                                 start=True, stop=True, skip_group_check=True)
                    ins = None
                    for tb in range(4):
                        ins = t.matmul(ps_b2[:, tb * 512:(tb + 1) * 512], onest[32:33, :],
                                       stats[32:33, tb * 512:(tb + 1) * 512],
                                       start=True, stop=True, skip_group_check=True)
                    M.inc(ins, "pe", 1)
                T(ft)
                bc_pe = M.val("pe")

                def fy(y):
                    M.wait(y, "dg", xload_dg)
                    M.wait(y, "dy", x_cur_dy)
                    for d in range(DC):
                        if d >= 2:
                            M.wait(y, "dve", xn_dve[d - 2])
                        M.inc(y.dma_start(out=xring[:, (d % 2) * TOK:(d % 2 + 1) * TOK],
                                          in_=x_d[d * 128:(d + 1) * 128, :]), "dy", 16)
                        dy_ld[d] = M.val("dy")
                Y(fy)

                def fv(v):
                    M.wait(v, "pe", bc_pe)
                    for d in range(DC):
                        M.wait(v, "dy", dy_ld[d])
                        v.tensor_tensor(xn_sb[:, d * TOK:(d + 1) * TOK],
                                        xring[:, (d % 2) * TOK:(d % 2 + 1) * TOK],
                                        ps_b1[:], ALU.mult)
                        got = M.inc(v.tensor_tensor(xn_sb[:, d * TOK:(d + 1) * TOK],
                                                    xn_sb[:, d * TOK:(d + 1) * TOK],
                                                    ps_b2[:], ALU.add), "dve", 1)
                        assert got == xn_dve[d]
                V(fv)

        def emit_attn(cfg):
            sub = SUBA[0]
            emit_stats()
            if sub < 2:
                barrier()
                return
            with nc.sbuf_tensor(_nm("xn_sb"), [128, DC * TOK], BF16) as xn_sb:
                emit_xn(xn_sb)
                if sub < 3:
                    barrier()
                    return

                kv_oc = cfg["kv_oc"]

                def f_fence_g(g):
                    M.wait(g, "dy")
                G(f_fence_g)
                with nc.sbuf_tensor(_nm("wstr"), [128, 2 * 2048], BF16) as wstr, \
                     nc.sbuf_tensor(_nm("kst"), [128, 4 * 512], BF16) as kst:
                    with nc.psum_tensor(_nm("ps_k"), [128, 512], F32) as ps_k, \
                         nc.psum_tensor(_nm("ps_k2"), [128, 512], F32) as ps_k2:
                        kp_pe = {}
                        kjob = 0
                        for oc in range(kv_oc):
                            b = oc % 2

                            def fg(g, oc=oc, b=b):
                                if oc >= 2:
                                    M.wait(g, "pe", kp_pe[oc - 2])
                                M.inc(g.dma_start(out=wstr[:, b * 2048:(b + 1) * 2048],
                                                  in_=cfg["wk"][oc]), "dg", 16)
                            G(fg)
                            for tb in range(4):
                                pk = ps_k if tb % 2 == 0 else ps_k2
                                sb4 = kjob % 4
                                dve_b = M.val("dve")
                                dy_b = M.val("dy")

                                def ft(t, b=b, tb=tb, pk=pk, dve_b=dve_b, oc=oc):
                                    M.wait(t, "dg")
                                    M.wait(t, "dve", dve_b)
                                    ins = None
                                    for d in range(DC):
                                        ins = t.matmul(pk[:],
                                                       wstr[:, b * 2048 + d * 128:
                                                            b * 2048 + (d + 1) * 128],
                                                       xn_sb[:, d * TOK + tb * 512:
                                                             d * TOK + (tb + 1) * 512],
                                                       start=(d == 0), stop=(d == DC - 1))
                                    kp_pe[oc] = M.inc(ins, "pe", 1)
                                T(ft)

                                def fv(v, pk=pk, sb4=sb4, dy_b=dy_b):
                                    M.wait(v, "pe")
                                    M.wait(v, "dy", dy_b)
                                    if cfg["kb"] is not None:
                                        ins = v.tensor_scalar(kst[:, sb4 * 512:(sb4 + 1) * 512],
                                                              pk[:], cfg["kb"][:, 0:1], None,
                                                              ALU.add)
                                    else:
                                        ins = v.tensor_copy(kst[:, sb4 * 512:(sb4 + 1) * 512],
                                                            pk[:])
                                    M.inc(ins, "dve", 1)
                                V(fv)

                                def fy(y, oc=oc, tb=tb, sb4=sb4):
                                    M.wait(y, "dve")
                                    M.inc(y.dma_start(
                                        out=cfg["agk_i"][oc * 128:(oc + 1) * 128,
                                                         tb * 512:(tb + 1) * 512],
                                        in_=kst[:, sb4 * 512:(sb4 + 1) * 512]), "dy", 16)
                                Y(fy)
                                kjob += 1

                    if DBG[0] and cfg["tag"] == "a":
                        def fdw(y):
                            M.wait(y, "pe")
                            M.inc(y.dma_start(out=DBGW[0], in_=wstr[:, 0:2048]),
                                  "dy", 16)
                        Y(fdw)

                    ndv = max(1, cfg["dv"] // 512)
                    dvw = min(cfg["dv"], 512)
                    with nc.sbuf_tensor(_nm("vsl"), [128, 2 * 16 * 512], BF16) as vsl, \
                         ExitStack() as es_v:
                        ps_v = [es_v.enter_context(
                            nc.psum_tensor(_nm(f"ps_v{i}"), [128, 512], F32)) for i in range(8)]
                        vp_evac = {}
                        vp_pe = {}
                        kv_evac_all = M.val("dve")
                        for dvs in range(ndv):
                            b = dvs % 2

                            def fg(g, dvs=dvs, b=b):
                                if dvs >= 2:
                                    M.wait(g, "pe", vp_pe[dvs - 2])
                                for d in range(DC):
                                    if cfg["tag"] == "a":
                                        src = cfg["wv"][0][:, d * 128:(d + 1) * 128]
                                    else:
                                        src = cfg["wv"][d * 128:(d + 1) * 128,
                                                        dvs * 512:(dvs + 1) * 512]
                                    M.inc(g.dma_start(
                                        out=vsl[:, b * 8192 + d * 512:
                                                b * 8192 + d * 512 + dvw],
                                        in_=src), "dg", 16)
                            G(fg)
                            for tth in range(2):
                                prev_ev = vp_evac.get((dvs, tth - 1),
                                                      vp_evac.get((dvs - 1, 1),
                                                                  kv_evac_all))

                                def ft(t, dvs=dvs, b=b, tth=tth, prev_ev=prev_ev):
                                    M.wait(t, "dg")
                                    if prev_ev:
                                        M.wait(t, "dve", prev_ev)
                                    ins = None
                                    for d in range(DC):
                                        for tt in range(8):
                                            ins = t.matmul(
                                                ps_v[tt][:, 0:dvw],
                                                xn_sb[:, d * TOK + (tth * 8 + tt) * 128:
                                                      d * TOK + (tth * 8 + tt) * 128 + 128],
                                                vsl[:, b * 8192 + d * 512:
                                                    b * 8192 + d * 512 + dvw],
                                                start=(d == 0), stop=(d == DC - 1),
                                                skip_group_check=True)
                                    vp_pe[dvs] = M.inc(ins, "pe", 1)
                                T(ft)
                                for tt in range(8):
                                    dy_b = M.val("dy")

                                    def fev(v, tt=tt, dy_b=dy_b):
                                        M.wait(v, "pe")
                                        M.wait(v, "dy", dy_b)
                                        if cfg["vb"]:
                                            ins = v.tensor_tensor(
                                                kst[:, (tt % 4) * 512:(tt % 4) * 512 + dvw],
                                                ps_v[tt][:, 0:dvw], vb_t[:, 0:dvw], ALU.add)
                                        else:
                                            ins = v.tensor_copy(
                                                kst[:, (tt % 4) * 512:(tt % 4) * 512 + dvw],
                                                ps_v[tt][:, 0:dvw])
                                        ev = M.inc(ins, "dve", 1)
                                        return ev
                                    V(fev)
                                    vp_evac[(dvs, tth)] = M.val("dve")

                                    def fy(y, tt=tt, dvs=dvs, tth=tth):
                                        M.wait(y, "dve")
                                        M.inc(y.dma_start(
                                            out=cfg["agv_i"][(tth * 8 + tt) * 128:
                                                             (tth * 8 + tt + 1) * 128,
                                                             dvs * 512:dvs * 512 + dvw],
                                            in_=kst[:, (tt % 4) * 512:(tt % 4) * 512 + dvw]),
                                            "dy", 16)
                                    Y(fy)

                if sub < 4:
                    barrier()
                    return

                def fcoll(g):
                    M.wait(g, "dy")
                    M.inc(g.collective_compute(
                        "AllGather", ALU.bypass, replica_groups=RG,
                        ins=[cfg["agk_i"].opt()], outs=[cfg["agk_o"].opt()]),
                        cfg["ck"], 1)
                    M.inc(g.collective_compute(
                        "AllGather", ALU.bypass, replica_groups=RG,
                        ins=[cfg["agv_i"].opt()], outs=[cfg["agv_o"].opt()]),
                        cfg["cv"], 1)
                G(fcoll)

                if sub < 5:
                    barrier()
                    return

                with nc.sbuf_tensor(_nm("kh"), [128, 2 * SK], BF16) as kh, \
                     nc.sbuf_tensor(_nm("vh"), [128, 2 * SK], BF16) as vh, \
                     nc.sbuf_tensor(_nm("qh"), [128, 2 * TOK], BF16) as qh, \
                     nc.sbuf_tensor(_nm("scr"), [128, 4096], BF16) as scr, \
                     nc.sbuf_tensor(_nm("wqs"), [128, 2048], BF16) as wqs, \
                     nc.sbuf_tensor(_nm("ost"), [128, 2 * 512], BF16) as ost, \
                     nc.psum_tensor(_nm("ps_O"), [128, 512], F32) as ps_O, \
                     nc.psum_tensor(_nm("ps_rs"), [1, 512], F32) as ps_rs, \
                     nc.psum_tensor(_nm("ps_mi"), [128, 512], F32) as ps_mi, \
                     nc.psum_tensor(_nm("ps_cl"), [128, 1], F32) as ps_cl:
                    share = cfg["share_kv"]
                    qp_pe = {}
                    head_pe = {}
                    head_act = {}
                    exp_act = {}
                    av_pe = {}
                    odma = []
                    ri = [0]
                    for h in range(NH):
                        hb = 0 if share else h % 2
                        if (not share) or h == 0:
                            pe_b = head_pe.get(h - 2, 0)
                            act_b = head_act.get(h - 2, 0)

                            def fy(y, h=h, hb=hb, pe_b=pe_b, act_b=act_b):
                                if h == 0:
                                    M.wait(y, "dg")
                                M.wait(y, cfg["ck"], M.val(cfg["ck"]))
                                M.wait(y, cfg["cv"], M.val(cfg["cv"]))
                                if pe_b:
                                    M.wait(y, "pe", pe_b)
                                if act_b:
                                    M.wait(y, "act", act_b)
                                for r in range(2):
                                    if share:
                                        ksrc = cfg["agk_o"][r * 128:(r + 1) * 128, :]
                                    else:
                                        ksrc = cfg["agk_o"][r * D + h * 128:
                                                            r * D + (h + 1) * 128, :]
                                    M.inc(y.dma_start(out=kh[:, hb * SK + r * TOK:
                                                             hb * SK + (r + 1) * TOK],
                                                      in_=ksrc), "dy", 16)
                                for r in range(2):
                                    if share:
                                        vsrc = cfg["agv_o"].rearrange(
                                            "(r t p) j -> r p t j", r=2, p=128)[r]
                                    else:
                                        vsrc = cfg["agv_o"].rearrange(
                                            "(r t p) (hh j) -> r hh p t j",
                                            r=2, p=128, j=128)[r, h]
                                    M.inc(y.dma_start(
                                        out=vh[:, hb * SK + r * TOK:hb * SK + (r + 1) * TOK]
                                        .rearrange("p (t j) -> p t j", j=128),
                                        in_=vsrc), "dy", 16)
                            Y(fy)
                        kv_dy = M.val("dy")

                        pe_b = qp_pe.get(h - 1, 0)

                        def fg(g, h=h, pe_b=pe_b):
                            if h == 0:
                                M.wait(g, "pe")
                            elif pe_b:
                                M.wait(g, "pe", pe_b)
                            M.inc(g.dma_start(out=wqs[:, 0:2048],
                                              in_=cfg["wq"][h]), "dg", 16)
                        G(fg)
                        for tb in range(4):
                            dve_b = M.val("dve")

                            def ft(t, h=h, tb=tb, dve_b=dve_b):
                                M.wait(t, "dg")
                                if tb == 0:
                                    M.wait(t, "dve", dve_b)
                                    M.wait(t, "act")
                                else:
                                    M.wait(t, "dve")
                                ins = None
                                for d in range(DC):
                                    ins = t.matmul(ps_mi[:],
                                                   wqs[:, d * 128:(d + 1) * 128],
                                                   xn_sb[:, d * TOK + tb * 512:
                                                         d * TOK + (tb + 1) * 512],
                                                   start=(d == 0), stop=(d == DC - 1))
                                qp_pe[h] = M.inc(ins, "pe", 1)
                            T(ft)

                            def fv(v, h=h, tb=tb):
                                M.wait(v, "pe")
                                if cfg["qb"] is not None:
                                    ins = v.tensor_scalar(qh[:, (h % 2) * TOK + tb * 512:
                                                             (h % 2) * TOK + (tb + 1) * 512],
                                                          ps_mi[:], cfg["qb"][:, h:h + 1],
                                                          None, ALU.add)
                                else:
                                    ins = v.tensor_copy(qh[:, (h % 2) * TOK + tb * 512:
                                                           (h % 2) * TOK + (tb + 1) * 512],
                                                        ps_mi[:])
                                M.inc(ins, "dve", 1)
                            V(fv)

                        if DBG[0] and h == 0:
                            def fdbg(y, h=h):
                                M.wait(y, "dve")
                                M.inc(y.dma_start(out=QH_CELL[0],
                                                  in_=qh[:, (h % 2) * TOK:
                                                         (h % 2 + 1) * TOK]), "dy", 16)
                            Y(fdbg)

                        with nc.psum_tensor(_nm("ps_n"), [1, 2048], F32) as ps_n:
                            def fs(s, h=h):
                                M.wait(s, "dve")
                                M.inc(s.activation(scr[:, 0:2048],
                                                   qh[:, (h % 2) * TOK:(h % 2 + 1) * TOK],
                                                   AF.Square), "act", 1)
                            SC(fs)

                            def ft(t):
                                M.wait(t, "act")
                                ins = None
                                for tb in range(4):
                                    ins = t.matmul(ps_n[:, tb * 512:(tb + 1) * 512], onesb[:],
                                                   scr[:, tb * 512:(tb + 1) * 512],
                                                   start=True, stop=True,
                                                   skip_group_check=True)
                                M.inc(ins, "pe", 1)
                            T(ft)

                            def fv(v):
                                M.wait(v, "pe")
                                M.inc(v.reduce_max(sc_sb[0:1, 0:1], ps_n[0:1, :], axis=AXX),
                                      "dve", 1)
                            V(fv)

                            if (not share) or h == 0:
                                for half in range(2):
                                    def fs2(s, hb=hb, half=half, kv_dy=kv_dy):
                                        M.wait(s, "dy", kv_dy)
                                        M.wait(s, "dve")
                                        M.inc(s.activation(scr[:, 2048:4096],
                                                           kh[:, hb * SK + half * TOK:
                                                              hb * SK + (half + 1) * TOK],
                                                           AF.Square), "act", 1)
                                    SC(fs2)

                                    def ft2(t):
                                        M.wait(t, "act")
                                        ins = None
                                        for tb in range(4):
                                            ins = t.matmul(ps_n[:, tb * 512:(tb + 1) * 512],
                                                           onesb[:],
                                                           scr[:, 2048 + tb * 512:
                                                               2048 + (tb + 1) * 512],
                                                           start=True, stop=True,
                                                           skip_group_check=True)
                                        M.inc(ins, "pe", 1)
                                    T(ft2)

                                    def fv2(v, half=half):
                                        M.wait(v, "pe")
                                        M.inc(v.reduce_max(sc_sb[0:1, 1 + half:2 + half],
                                                           ps_n[0:1, :], axis=AXX), "dve", 1)
                                    V(fv2)

                                def fv3(v):
                                    M.inc(v.tensor_tensor(sc_sb[0:1, 5:6], sc_sb[0:1, 1:2],
                                                          sc_sb[0:1, 2:3], ALU.max), "dve", 1)
                                V(fv3)

                            def fv4(v):
                                v.tensor_tensor(sc_sb[0:1, 4:5], sc_sb[0:1, 0:1],
                                                sc_sb[0:1, 5:6], ALU.add)
                                M.inc(v.tensor_scalar(sc_sb[0:1, 4:5], sc_sb[0:1, 4:5],
                                                      -0.5 * CSC, None, ALU.mult), "dve", 1)
                            V(fv4)

                            def ft3(t):
                                M.wait(t, "dve")
                                M.wait(t, "act")
                                M.inc(t.matmul(ps_cl[:], onest[0:1, :], sc_sb[0:1, 4:5],
                                               start=True, stop=True), "pe", 1)
                            T(ft3)

                            def fs3(s):
                                M.wait(s, "pe")
                                M.inc(s.copy(ncb[:], ps_cl[:]), "act", 1)
                            SC(fs3)

                            def fv5(v):
                                M.wait(v, "act")
                                ins = None
                                for s4 in range(4):
                                    ins = v.tensor_scalar(btbl[:, s4 * 16:(s4 + 1) * 16],
                                                          mcol_s[:, s4 * 16:(s4 + 1) * 16],
                                                          ncb[:, 0:1], None, ALU.add)
                                M.inc(ins, "dve", 1)
                            V(fv5)
                        btbl_dve = M.val("dve")

                        with nc.psum_tensor(_nm("ps_sA"), [128, 1024], F32) as ps_sA, \
                             nc.psum_tensor(_nm("ps_sB"), [128, 1024], F32) as ps_sB:
                            for s4 in range(4):
                                groups = (list(range(0, 2 * s4 + 2))
                                          + list(range(8, 8 + 2 * s4 + 2)))
                                cand = {2 * s4: ("a", 0), 2 * s4 + 1: ("a", 1),
                                        8 + 2 * s4: ("b", 0), 8 + 2 * s4 + 1: ("b", 1)}
                                slot_pre_dve = M.val("dve")
                                r0 = ri[0]
                                ng = len(groups)
                                act_base = M.val("act")
                                dve_base = M.val("dve")
                                mask_val = {}
                                kc = 0
                                for gi, g in enumerate(groups):
                                    exp_act[r0 + gi] = act_base + gi + 1
                                    if g in cand:
                                        kc += 1
                                        mask_val[r0 + gi] = dve_base + kc
                                score_pe = {}

                                def ftall(t, h=h, s4=s4, groups=groups, r0=r0,
                                          kv_dy=kv_dy, hb=hb, btbl_dve=btbl_dve,
                                          slot_pre_dve=slot_pre_dve, cand=cand,
                                          mask_val=mask_val, ng=ng, score_pe=score_pe):
                                    def av(gi, g):
                                        r = r0 + gi
                                        if r in mask_val:
                                            M.wait(t, "dve", mask_val[r])
                                        else:
                                            M.wait(t, "act", exp_act[r])
                                        if gi == 0:
                                            M.wait(t, "dve", slot_pre_dve)
                                        first = gi == 0
                                        last = gi == ng - 1
                                        for j in range(2):
                                            kt = 2 * g + j
                                            t.matmul(ps_O[:],
                                                     vh[:, hb * SK + kt * 128:
                                                        hb * SK + (kt + 1) * 128],
                                                     scr[:, (r % 4) * 1024 + j * 512:
                                                         (r % 4) * 1024 + (j + 1) * 512],
                                                     start=(first and j == 0),
                                                     stop=(last and j == 1),
                                                     skip_group_check=True)
                                        ins = None
                                        for j in range(2):
                                            ins = t.matmul(ps_rs[:], onesb[:],
                                                           scr[:, (r % 4) * 1024 + j * 512:
                                                               (r % 4) * 1024 + (j + 1) * 512],
                                                           start=(first and j == 0),
                                                           stop=(last and j == 1),
                                                           skip_group_check=True)
                                        av_pe[r] = M.inc(ins, "pe", 1)

                                    for gi, g in enumerate(groups):
                                        r = r0 + gi
                                        if gi == 0:
                                            M.wait(t, "dy", kv_dy)
                                            M.wait(t, "dve", btbl_dve)
                                        if exp_act.get(r - 2, 0):
                                            M.wait(t, "act", exp_act[r - 2])
                                        psS = ps_sA if r % 2 == 0 else ps_sB
                                        ins = None
                                        for j in range(2):
                                            kt = 2 * g + j
                                            ins = t.matmul(psS[:, j * 512:(j + 1) * 512],
                                                           kh[:, hb * SK + kt * 128:
                                                              hb * SK + (kt + 1) * 128],
                                                           qh[:, (h % 2) * TOK + s4 * 512:
                                                              (h % 2) * TOK + (s4 + 1) * 512],
                                                           start=True, stop=True,
                                                           skip_group_check=True)
                                        score_pe[r] = M.inc(ins, "pe", 1)
                                        if gi >= 1:
                                            av(gi - 1, groups[gi - 1])
                                    av(ng - 1, groups[ng - 1])
                                T(ftall)

                                def fsall(s, s4=s4, groups=groups, r0=r0,
                                          score_pe=score_pe):
                                    for gi, g in enumerate(groups):
                                        r = r0 + gi
                                        psS = ps_sA if r % 2 == 0 else ps_sB
                                        M.wait(s, "pe", score_pe[r])
                                        if r - 4 in av_pe:
                                            M.wait(s, "pe", av_pe[r - 4])
                                        got = M.inc(
                                            s.activation(scr[:, (r % 4) * 1024:
                                                             (r % 4 + 1) * 1024],
                                                         psS[:], AF.Exp,
                                                         bias=btbl[:, s4 * 16 + g:
                                                                   s4 * 16 + g + 1],
                                                         scale=CSC), "act", 1)
                                        assert got == exp_act[r], (got, exp_act[r])
                                SC(fsall)

                                def fvall(v, groups=groups, r0=r0, cand=cand,
                                          mask_val=mask_val):
                                    for gi, g in enumerate(groups):
                                        if g not in cand:
                                            continue
                                        r = r0 + gi
                                        which, half = cand[g]
                                        msk = ma_s if which == "a" else mb_s
                                        M.wait(v, "act", exp_act[r])
                                        ins = None
                                        for j in range(2):
                                            ins = v.tensor_tensor(
                                                scr[:, (r % 4) * 1024 + j * 512:
                                                    (r % 4) * 1024 + (j + 1) * 512],
                                                scr[:, (r % 4) * 1024 + j * 512:
                                                    (r % 4) * 1024 + (j + 1) * 512],
                                                msk[:, (half * 2 + j) * 512:
                                                    (half * 2 + j + 1) * 512],
                                                ALU.mult)
                                        got = M.inc(ins, "dve", 1)
                                        assert got == mask_val[r], (got, mask_val[r])
                                V(fvall)
                                ri[0] += ng

                                def fvR(v):
                                    M.wait(v, "pe")
                                    M.inc(v.reciprocal(rec_sb[0:1, :], ps_rs[:]), "dve", 1)
                                V(fvR)

                                def ftB(t):
                                    M.wait(t, "dve")
                                    M.inc(t.matmul(ps_mi[:], onest[0:1, :], rec_sb[0:1, :],
                                                   start=True, stop=True), "pe", 1)
                                T(ftB)

                                def fsB(s):
                                    M.wait(s, "pe")
                                    M.inc(s.copy(recb[:], ps_mi[:]), "act", 1)
                                SC(fsB)

                                ost_k = len(odma)
                                dy_b = odma[ost_k - 2] if ost_k >= 2 else 0

                                def fvO(v, s4=s4, dy_b=dy_b):
                                    M.wait(v, "act")
                                    if dy_b:
                                        M.wait(v, "dy", dy_b)
                                    M.inc(v.tensor_tensor(ost[:, (s4 % 2) * 512:
                                                              (s4 % 2 + 1) * 512],
                                                          ps_O[:], recb[:], ALU.mult),
                                          "dve", 1)
                                V(fvO)

                                def fyO(y, h=h, s4=s4):
                                    M.wait(y, "dve")
                                    odma.append(M.inc(y.dma_start(
                                        out=o_dram[h * 128:(h + 1) * 128,
                                                   s4 * 512:(s4 + 1) * 512],
                                        in_=ost[:, (s4 % 2) * 512:(s4 % 2 + 1) * 512]),
                                        "dy", 16))
                                Y(fyO)
                        head_pe[h] = M.val("pe")
                        head_act[h] = M.val("act")
                        if cfg["tag"] == "a" and not _os.environ.get("NODQ"):
                            emit_dq_jobs(8)

            if sub < 6:
                barrier()
                return

            barrier()
            with nc.sbuf_tensor(_nm("wo_sb"), [128, 16 * D], BF16) as wo_sb, \
                 nc.sbuf_tensor(_nm("osl"), [128, 16 * 512], BF16) as osl, \
                 nc.sbuf_tensor(_nm("xio"), [128, 2 * 8192], BF16) as xio, \
                 ExitStack() as es_o:
                ps_d = [es_o.enter_context(nc.psum_tensor(_nm(f"ps_d{i}"), [128, 512], F32))
                        for i in range(4)]

                def fgo(g):
                    for q in range(4):
                        M.inc(g.dma_start(out=wo_sb[:, q * 8192:(q + 1) * 8192],
                                          in_=cfg["wo"][:, q * 8192:(q + 1) * 8192]),
                              "dg", 16)
                G(fgo)
                dve_evac = {}
                st_dy = {}
                for s4 in range(4):
                    xb = (s4 % 2) * 8192

                    def fy(y, s4=s4, xb=xb):
                        M.wait(y, "dy")
                        if s4 > 0:
                            M.wait(y, "pe")
                        if s4 >= 2:
                            M.wait(y, "dy", st_dy[s4 - 2])
                        M.inc(y.dma_start(
                            out=osl[:].rearrange("p (hh tk) -> p hh tk", tk=512),
                            in_=o_dram.rearrange("(hh p) tk -> p hh tk", p=128)
                                [:, :, s4 * 512:(s4 + 1) * 512]), "dy", 16)
                        M.inc(y.dma_start(
                            out=xio[:, xb:xb + 8192].rearrange("p (dc q) -> p dc q", q=512),
                            in_=x_d.rearrange("(dc p) tk -> p dc tk", p=128)
                                [:, :, s4 * 512:(s4 + 1) * 512]), "dy", 16)
                    Y(fy)
                    for dc in range(DC):
                        pb = ps_d[dc % 4]

                        def ft(t, s4=s4, dc=dc, pb=pb):
                            M.wait(t, "dg")
                            M.wait(t, "dy")
                            k = s4 * DC + dc
                            if k >= 4:
                                M.wait(t, "dve", dve_evac[k - 4])
                            ins = None
                            for hh in range(NH):
                                ins = t.matmul(pb[:],
                                               wo_sb[:, (hh * 16 + dc) * 128:
                                                     (hh * 16 + dc + 1) * 128],
                                               osl[:, hh * 512:(hh + 1) * 512],
                                               start=(hh == 0), stop=(hh == NH - 1))
                            M.inc(ins, "pe", 1)
                        T(ft)

                        def fv(v, s4=s4, dc=dc, pb=pb, xb=xb):
                            M.wait(v, "pe")
                            dve_evac[s4 * DC + dc] = M.inc(v.scalar_tensor_tensor(
                                xio[:, xb + dc * 512:xb + (dc + 1) * 512], pb[:],
                                cfg["ob"][:, dc:dc + 1],
                                xio[:, xb + dc * 512:xb + (dc + 1) * 512],
                                ALU.add, ALU.add), "dve", 1)
                        V(fv)

                    def fyS(y, s4=s4, xb=xb):
                        M.wait(y, "dve", dve_evac[s4 * DC + DC - 1])
                        st_dy[s4] = M.inc(y.dma_start(
                            out=x_d.rearrange("(dc p) tk -> p dc tk", p=128)
                                [:, :, s4 * 512:(s4 + 1) * 512],
                            in_=xio[:, xb:xb + 8192].rearrange("p (dc q) -> p dc q",
                                                               q=512)), "dy", 16)
                    Y(fyS)
            barrier()

        ffn_st_dy = {}

        def emit_ffn(li):
            emit_stats()
            with nc.sbuf_tensor(_nm("xnf"), [128, DC * 512], BF16) as xnf, \
                 nc.sbuf_tensor(_nm("hsb"), [128, OCH * 512], BF16) as hsb, \
                 nc.sbuf_tensor(_nm("silu"), [128, 2 * 512], BF16) as silu, \
                 nc.sbuf_tensor(_nm("wgs"), [128, 4 * 2048], BF16) as wgs, \
                 nc.sbuf_tensor(_nm("wds"), [128, 2 * HID], BF16) as wds, \
                 nc.sbuf_tensor(_nm("xio"), [128, 2 * 8192], BF16) as xio:

                def fdq(g):
                    M.wait(g, "dy")
                G(fdq)
                for tb in range(4):
                    xb = (tb % 2) * 8192
                    x_cur_dy = M.val("dy")

                    def fyL(y, tb=tb, xb=xb, x_cur_dy=x_cur_dy, li=li):
                        M.wait(y, "dy", x_cur_dy)
                        if tb >= 2:
                            M.wait(y, "dy", ffn_st_dy[(li, tb - 2)])
                        M.inc(y.dma_start(
                            out=xio[:, xb:xb + 8192].rearrange("p (dc q) -> p dc q",
                                                               q=512),
                            in_=x_d.rearrange("(dc p) tk -> p dc tk", p=128)
                                [:, :, tb * 512:(tb + 1) * 512]), "dy", 16)
                    Y(fyL)
                    xld_dy = M.val("dy")

                    with nc.psum_tensor(_nm("ps_b1"), [128, 512], F32) as ps_b1, \
                         nc.psum_tensor(_nm("ps_b2"), [128, 512], F32) as ps_b2:
                        dve_prev = M.val("dve")

                        def ft(t, tb=tb, dve_prev=dve_prev):
                            M.wait(t, "dve", dve_prev)
                            M.wait(t, "act")
                            t.matmul(ps_b1[:], onest[0:1, :],
                                     stats[0:1, tb * 512:(tb + 1) * 512],
                                     start=True, stop=True)
                            M.inc(t.matmul(ps_b2[:], onest[32:33, :],
                                           stats[32:33, tb * 512:(tb + 1) * 512],
                                           start=True, stop=True), "pe", 1)
                        T(ft)

                        def fv(v, tb=tb, xb=xb, xld_dy=xld_dy):
                            M.wait(v, "pe")
                            M.wait(v, "dy", xld_dy)
                            for d in range(DC):
                                v.tensor_tensor(xnf[:, d * 512:(d + 1) * 512],
                                                xio[:, xb + d * 512:xb + (d + 1) * 512],
                                                ps_b1[:], ALU.mult)
                            ins = None
                            for d in range(DC):
                                ins = v.tensor_tensor(xnf[:, d * 512:(d + 1) * 512],
                                                      xnf[:, d * 512:(d + 1) * 512],
                                                      ps_b2[:], ALU.add)
                            M.inc(ins, "dve", 1)
                        V(fv)
                    xnf_dve = M.val("dve")

                    with nc.psum_tensor(_nm("ps_G"), [128, 2 * 512], F32) as ps_G, \
                         nc.psum_tensor(_nm("ps_U"), [128, 2 * 512], F32) as ps_U:
                        silu_done = {}
                        hev_done = {}
                        gu_pe = {}
                        for c0 in range(0, OCH, 8):
                            ocs = list(range(c0, min(c0 + 8, OCH)))
                            act_b = M.val("act")
                            dve_b = M.val("dve")
                            for i, oc in enumerate(ocs):
                                silu_done[oc] = act_b + i + 1
                                hev_done[oc] = dve_b + i + 1
                            for oc in ocs:
                                b = oc % 2

                                def fg(g, oc=oc, b=b):
                                    if oc >= 2:
                                        M.wait(g, "pe", gu_pe[oc - 2])
                                    M.inc(g.dma_start(out=wgs[:, (2 * b) * 2048:
                                                              (2 * b + 1) * 2048],
                                                      in_=wg_c[oc]), "dg", 16)
                                    M.inc(g.dma_start(out=wgs[:, (2 * b + 1) * 2048:
                                                              (2 * b + 2) * 2048],
                                                      in_=wu_c[oc]), "dg", 16)
                                G(fg)

                                def ft(t, oc=oc, b=b, xnf_dve=xnf_dve):
                                    M.wait(t, "dg")
                                    M.wait(t, "dve", xnf_dve)
                                    if oc >= 2:
                                        M.wait(t, "act", silu_done[oc - 2])
                                        M.wait(t, "dve", hev_done[oc - 2])
                                    for d in range(DC):
                                        t.matmul(ps_G[:, b * 512:(b + 1) * 512],
                                                 wgs[:, (2 * b) * 2048 + d * 128:
                                                     (2 * b) * 2048 + (d + 1) * 128],
                                                 xnf[:, d * 512:(d + 1) * 512],
                                                 start=(d == 0), stop=(d == DC - 1),
                                                 skip_group_check=True)
                                    ins = None
                                    for d in range(DC):
                                        ins = t.matmul(ps_U[:, b * 512:(b + 1) * 512],
                                                       wgs[:, (2 * b + 1) * 2048 + d * 128:
                                                           (2 * b + 1) * 2048 + (d + 1) * 128],
                                                       xnf[:, d * 512:(d + 1) * 512],
                                                       start=(d == 0), stop=(d == DC - 1),
                                                       skip_group_check=True)
                                    gu_pe[oc] = M.inc(ins, "pe", 1)
                                T(ft)

                            def fs_ch(s, ocs=ocs):
                                for oc in ocs:
                                    b = oc % 2
                                    M.wait(s, "pe", gu_pe[oc])
                                    got = M.inc(
                                        s.activation(silu[:, b * 512:(b + 1) * 512],
                                                     ps_G[:, b * 512:(b + 1) * 512],
                                                     AF.Silu, scale=sg_s[:, oc:oc + 1]),
                                        "act", 1)
                                    assert got == silu_done[oc]
                            SC(fs_ch)

                            def fv_ch(v, ocs=ocs):
                                for oc in ocs:
                                    b = oc % 2
                                    M.wait(v, "act", silu_done[oc])
                                    got = M.inc(v.scalar_tensor_tensor(
                                        hsb[:, oc * 512:(oc + 1) * 512],
                                        ps_U[:, b * 512:(b + 1) * 512],
                                        su_s[:, oc:oc + 1],
                                        silu[:, b * 512:(b + 1) * 512],
                                        ALU.mult, ALU.mult), "dve", 1)
                                    assert got == hev_done[oc]
                            V(fv_ch)

                    with ExitStack() as es_d:
                        ps_Dn = [es_d.enter_context(
                            nc.psum_tensor(_nm(f"ps_dn{i}"), [128, 512], F32)) for i in range(4)]
                        h_dve = M.val("dve")
                        dn_evac = {}
                        dn_pe = {}
                        for dc in range(DC):
                            b = dc % 2

                            def fg(g, dc=dc, b=b):
                                if dc >= 2:
                                    M.wait(g, "pe", dn_pe[dc - 2])
                                M.inc(g.dma_start(out=wds[:, b * HID:(b + 1) * HID],
                                                  in_=wd_c[dc]), "dg", 16)
                            G(fg)

                            def ft(t, dc=dc, b=b, h_dve=h_dve):
                                M.wait(t, "dg")
                                M.wait(t, "dve", h_dve)
                                if dc >= 4:
                                    M.wait(t, "dve", dn_evac[dc - 4])
                                ins = None
                                for oc in range(OCH):
                                    ins = t.matmul(ps_Dn[dc % 4][:],
                                                   wds[:, b * HID + oc * 128:
                                                       b * HID + (oc + 1) * 128],
                                                   hsb[:, oc * 512:(oc + 1) * 512],
                                                   start=(oc == 0), stop=(oc == OCH - 1))
                                dn_pe[dc] = M.inc(ins, "pe", 1)
                            T(ft)

                            def fv(v, dc=dc, tb=tb, xb=xb):
                                M.wait(v, "pe")
                                dn_evac[dc] = M.inc(v.scalar_tensor_tensor(
                                    xio[:, xb + dc * 512:xb + (dc + 1) * 512],
                                    ps_Dn[dc % 4][:],
                                    sd_s[:, dc:dc + 1],
                                    xio[:, xb + dc * 512:xb + (dc + 1) * 512],
                                    ALU.mult, ALU.add), "dve", 1)
                            V(fv)

                        def fyS(y, tb=tb, xb=xb, li=li):
                            M.wait(y, "dve", dn_evac[DC - 1])
                            ffn_st_dy[(li, tb)] = M.inc(y.dma_start(
                                out=x_d.rearrange("(dc p) tk -> p dc tk", p=128)
                                    [:, :, tb * 512:(tb + 1) * 512],
                                in_=xio[:, xb:xb + 8192].rearrange(
                                    "p (dc q) -> p dc q", q=512)), "dy", 16)
                        Y(fyS)
            barrier()

        cfg_a = dict(tag="a", kv_oc=1, dv=128, share_kv=True,
                     wq=aqw, wk=akw, wv=avw, wo=aow,
                     qb=aqb_s, kb=akb_s, vb=True, ob=aob_s,
                     agk_i=ag_ak_i, agk_o=ag_ak_o, agv_i=ag_av_i, agv_o=ag_av_o,
                     ck="cak", cv="cav")
        cfg_b = dict(tag="b", kv_oc=16, dv=2048, share_kv=False,
                     wq=bqw, wk=bkw, wv=bvw, wo=bow,
                     qb=None, kb=None, vb=False, ob=bbo_s,
                     agk_i=ag_bk_i, agk_o=ag_bk_o, agv_i=ag_bv_i, agv_o=ag_bv_o,
                     ck="cbk", cv="cbv")

        if upto >= 2:
            emit_attn(cfg_a)
        if upto >= 3:
            emit_ffn(0)
        if upto >= 4:
            emit_attn(cfg_b)
        if upto >= 5:
            emit_ffn(1)

        def p_store(g):
            M.wait(g, "dy")
            M.inc(g.dma_start(out=outT, in_=x_d), "dg", 16)
            if DBG[0]:
                M.inc(g.dma_start(out=dbg_o, in_=o_dram), "dg", 16)
                M.inc(g.dma_start(out=dbg_sc, in_=sc_sb[:]), "dg", 16)
                M.inc(g.dma_start(out=dbg_st, in_=stats[:]), "dg", 16)
                M.inc(g.dma_start(out=dbg_bt, in_=btbl[:]), "dg", 16)
                M.inc(g.dma_start(out=dbg_qh, in_=QH_CELL[0]), "dg", 16)
                M.inc(g.dma_start(out=dbg_kh[:, 0:TOK], in_=ag_ak_o[0:128, :]), "dg", 16)
                M.inc(g.dma_start(out=dbg_kh[:, TOK:SK], in_=ag_ak_o[128:256, :]), "dg", 16)
                M.inc(g.dma_start(out=dbg_vh[:, 0:TOK], in_=ag_ak_i), "dg", 16)
                M.inc(g.dma_start(out=dbg_vh[:, TOK:SK], in_=DBGW[0]), "dg", 16)
            M.wait(g, "dg")
        G(p_store)

    return nc


def kernel(**inputs):
    f32 = np.float32
    x = np.asarray(inputs["L0_attn_a_hidden_states"], f32)
    aq = np.ascontiguousarray(np.asarray(inputs["L0_attn_a_q_w"], f32).T)
    ak = np.ascontiguousarray(np.asarray(inputs["L0_attn_a_k_w"], f32).T)
    av = np.ascontiguousarray(np.asarray(inputs["L0_attn_a_v_w"], f32).T)
    ao = np.ascontiguousarray(np.asarray(inputs["L0_attn_a_o_w"], f32).T)
    bq = np.ascontiguousarray(np.asarray(inputs["L0_attn_b_Wq"], f32).T)
    bk = np.ascontiguousarray(np.asarray(inputs["L0_attn_b_Wk"], f32).T)
    bv = np.ascontiguousarray(np.asarray(inputs["L0_attn_b_Wv"], f32).T)
    bo = np.ascontiguousarray(np.asarray(inputs["L0_attn_b_Wo"], f32).T)
    gqt = np.ascontiguousarray(np.asarray(inputs["L0_ffn_wq_gate"]).T.astype(np.int32))
    uqt = np.ascontiguousarray(np.asarray(inputs["L0_ffn_wq_up"]).T.astype(np.int32))
    dqt = np.ascontiguousarray(np.asarray(inputs["L0_ffn_wq_down"]).T.astype(np.int32))

    shared = {
        "aqw": _stripes(aq), "akw": _stripes(ak), "avw": _stripes(av),
        "aow": _woflat(ao),
        "bqw": _stripes(bq), "bkw": _stripes(bk), "bvw": bv, "bow": _woflat(bo),
        "gq": _stripes(gqt), "uq": _stripes(uqt), "dq": _stripes(dqt),
        "sg": _cols(np.asarray(inputs["L0_ffn_s_gate"], f32)),
        "su": _cols(np.asarray(inputs["L0_ffn_s_up"], f32)),
        "sd": _cols(np.asarray(inputs["L0_ffn_s_down"], f32)),
        "aqb": _cols(np.asarray(inputs["L0_attn_a_q_b"], f32)),
        "akb": np.ascontiguousarray(
            np.asarray(inputs["L0_attn_a_k_b"], f32).reshape(128, 1)),
        "avb": np.ascontiguousarray(
            np.asarray(inputs["L0_attn_a_v_b"], f32).reshape(1, 128)),
        "aob": _cols(np.asarray(inputs["L0_attn_a_o_b"], f32)),
        "bbo": _cols(np.asarray(inputs["L0_attn_b_bo"], f32)),
    }

    in_maps = []
    for c in range(8):
        b, h = c // 2, c % 2
        mcol, ma, mb = _masks(h)
        rows = np.concatenate([x[b, g * 512:(g + 1) * 512, :] for g in PERM[h]], 0)
        m = dict(shared)
        m["xT"] = np.ascontiguousarray(rows.T)
        m["mcol"] = mcol
        m["ma"] = np.ascontiguousarray(ma)
        m["mb"] = np.ascontiguousarray(mb)
        in_maps.append(m)

    nc = build_nc()
    res = run_bass_kernel_spmd(nc, in_maps, core_ids=list(range(8)))

    out = np.empty((4, 4096, 2048), np.float32)
    for c in range(8):
        b, h = c // 2, c % 2
        o = np.asarray(res.results[c]["outT"]).T
        for i, g in enumerate(PERM[h]):
            out[b, g * 512:(g + 1) * 512, :] = o[i * 512:(i + 1) * 512, :]
    return out
